# version 18
# baseline (speedup 1.0000x reference)
import numpy as np
import concourse.bass as bass
import concourse.mybir as mybir
from concourse.bass_utils import run_bass_kernel_spmd

F32 = mybir.dt.float32
BF16 = mybir.dt.bfloat16
AF = mybir.ActivationFunctionType
OP = mybir.AluOpType
AX = mybir.AxisListType

D = 4096
C = 32
NCORE = 8
TPC = 2048
NT = 512
NP = TPC // NT
SM = 13
INV_D = 1.0 / D
EPS = 1e-5


def a1_cnt(p, c):
    return (c + 1) if p == 0 else 96 + 128 * (p - 1) + c + 1


def a2_cnt(p, c):
    return (32 if p == 0 else 128 * p + 32) + c + 1


def a3_cnt(p, c):
    return a2_cnt(p, c) + 32


def a4_cnt(p, c):
    return (128 * (p + 1) if p < 3 else 480) + c + 1


def jsq(p, k, c):
    return 96 * p + 32 * (k - 2) + c


def build_nc():
    nc = bass.Bass("TRN2", target_bir_lowering=False, debug=False)
    x_d = nc.dram_tensor("x_fm", [D, TPC], F32, kind="ExternalInput").ap()
    w1_d = nc.dram_tensor("w1_r", [C, 128, D], F32, kind="ExternalInput").ap()
    w2t_d = nc.dram_tensor("w2t_r", [128, C * NT], F32, kind="ExternalInput").ap()
    msk_d = nc.dram_tensor("mask_slc", [D, NT], F32, kind="ExternalInput").ap()
    wbt_d = nc.dram_tensor("wbt_r", [C, 128, D], F32, kind="ExternalInput").ap()
    at_d = nc.dram_tensor("at_r", [128, D], F32, kind="ExternalInput").ap()
    bt_d = nc.dram_tensor("bt_r", [128, D], F32, kind="ExternalInput").ap()
    wrt_d = nc.dram_tensor("wrt_r", [128, 256], F32, kind="ExternalInput").ap()
    b2c_d = nc.dram_tensor("b2c", [128, C], F32, kind="ExternalInput").ap()
    bbc_d = nc.dram_tensor("bbc", [128, C], F32, kind="ExternalInput").ap()
    e16_d = nc.dram_tensor("e16", [8, 128], F32, kind="ExternalInput").ap()
    id_d = nc.dram_tensor("ident", [128, 128], F32, kind="ExternalInput").ap()
    out_d = nc.dram_tensor("out_fm", [D, TPC], F32, kind="ExternalOutput").ap()
    SPL = 20
    asl_a = nc.dram_tensor("aslice_a", [4 * 128, 128 * SPL], BF16, kind="Internal")
    asl_b = nc.dram_tensor("aslice_b", [4 * 128, 128 * (C - SPL)], BF16, kind="Internal")
    g1_d = nc.dram_tensor("gath1", [NCORE * 4 * 128, 128 * SPL], BF16, kind="Internal")
    g2_d = nc.dram_tensor("gath2", [NCORE * 4 * 128, 128 * (C - SPL)], BF16, kind="Internal")
    wbb_d = nc.dram_tensor("wbb", [C, 128, D], BF16, kind="Internal")

    from contextlib import ExitStack
    with ExitStack() as ctx:
        E = ctx.enter_context
        XP = [E(nc.sbuf_tensor(f"XP{i}", [128, C * NT], BF16)) for i in range(2)]
        U = E(nc.sbuf_tensor("U", [128, C * NT], BF16))
        AB = E(nc.sbuf_tensor("AB", [128, C * NT], BF16))
        SLA = E(nc.sbuf_tensor("SLA", [128, 2 * D], BF16))
        SLW = E(nc.sbuf_tensor("SLW", [128, 2 * D], BF16))
        SQ = E(nc.sbuf_tensor("SQ", [128, 2 * NT], BF16))
        SQ1 = E(nc.sbuf_tensor("SQ1", [128, 2 * NT], BF16))
        T1 = E(nc.sbuf_tensor("T1", [128, NT], BF16))
        OUTS = E(nc.sbuf_tensor("OUTS", [128, 4 * NT], F32))
        FEV = E(nc.sbuf_tensor("FEV", [128, 2 * NT], BF16))
        HW = E(nc.sbuf_tensor("HW", [128, NT], BF16))
        ARN = E(nc.sbuf_tensor("ARN", [128, 2048], F32))
        ATB = E(nc.sbuf_tensor("ATB", [128, D], BF16))
        BTB = E(nc.sbuf_tensor("BTB", [128, D], BF16))
        WRTB = E(nc.sbuf_tensor("WRTB", [128, 256], BF16))
        B2C = E(nc.sbuf_tensor("B2C", [128, C], F32))
        BBC = E(nc.sbuf_tensor("BBC", [128, C], F32))
        E16B = E(nc.sbuf_tensor("E16B", [8, 128], F32))
        IDB = E(nc.sbuf_tensor("IDB", [128, 128], F32))
        ONEC = E(nc.sbuf_tensor("ONEC", [128, 1], BF16))
        ONER = E(nc.sbuf_tensor("ONER", [1, 128], F32))
        MB = E(nc.sbuf_tensor("MB", [128, NT], BF16))
        RB = E(nc.sbuf_tensor("RB", [128, NT], BF16))
        PB = [E(nc.psum_tensor(f"PB{i}", [128, 512], F32)) for i in range(4)]
        PM = E(nc.psum_tensor("PM", [128, 512], F32))
        PR = E(nc.psum_tensor("PR", [128, 512], F32))
        PH = E(nc.psum_tensor("PH", [128, 512], F32))
        PX = E(nc.psum_tensor("PX", [128, 512], F32))
        sem_names = [
            "s_gsm", "s_ssm", "s_w2t", "s_fw1_0", "s_fw1_1", "s_fmm", "s_fev",
            "s_fst_0", "s_fst_1", "s_msk_0", "s_msk_1", "s_cc",
            "s_xin_0", "s_xin_1", "s_xin_2", "s_xin_3",
            "s_xin_4", "s_xin_5", "s_xin_6", "s_xin_7",
            "s_out_0", "s_out_1", "s_out_2", "s_out_3",
            "s_sla_0", "s_sla_1", "s_slw_0", "s_slw_1", "s_wbb", "s_ones",
            "s_cast", "s_sqr", "s_psq", "s_sqr1", "s_psq1",
            "s_fin", "s_rstd", "s_bc", "s_app", "s_mbc",
            "s_ymm", "s_evac", "s_rlg", "s_hmm", "s_hwv", "s_cwb",
            "s_tp1", "s_tp2", "s_rexp", "s_sm",
        ]
        S = {n: E(nc.semaphore(n)) for n in sem_names}
        block = E(nc.Block())

        mean = ARN[0:1, 0:512]
        var = ARN[0:1, 512:1024]
        msq = ARN[0:1, 1024:1536]
        rstd = ARN[0:1, 1536:2048]
        lgsb = ARN[0:8, 0:512]
        cfm = ARN[0:8, 512:1024]
        CW = ARN[:, 1024:1536]
        lgtm = ARN[:, 1536:1568]
        M1 = ARN[:, 1568:1572]
        M2 = ARN[:, 1572:1576]
        e2v = ARN[:, 1576:1580]
        rec = ARN[:, 1580:1584]
        lm = ARN[:, 1584:1616]
        mskS = ARN[:, 1616:1648]
        sel = ARN[:, 1648:1680]
        elm = ARN[:, 1680:1712]
        comb = ARN[:, 1712:1744]
        MB2 = FEV[:, 0:NT]
        RB2 = FEV[:, NT:2 * NT]
        MK = [OUTS[:, 0:NT], OUTS[:, NT:2 * NT]]

        def uch(c):
            return U[:, NT * c:NT * (c + 1)]

        def xpc(p, c):
            return XP[p % 2][:, NT * c:NT * (c + 1)]

        def abch(c):
            return AB[:, NT * c:NT * (c + 1)]

        def sla_t(b, s):
            return SLA[:, D * b + 128 * s: D * b + 128 * s + 128]

        def slw_t(b, s):
            return SLW[:, D * b + 128 * s: D * b + 128 * s + 128]

        def sqs(j):
            return SQ[:, NT * (j % 2): NT * (j % 2) + NT]

        def sq1s(c):
            return SQ1[:, NT * (c % 2): NT * (c % 2) + NT]

        def xin_sem(p, g):
            return S[f"s_xin_{(p % 2) * 4 + g}"]

        xr = x_d.rearrange("(c p) n -> p c n", p=128)
        outr = out_d.rearrange("(c p) n -> p c n", p=128)

        @block.gpsimd
        def _(g):
            g.dma_start(out=ATB[:, :], in_=at_d[:, :]).then_inc(S["s_gsm"], 16)
            g.dma_start(out=BTB[:, :], in_=bt_d[:, :]).then_inc(S["s_gsm"], 16)
            g.dma_start(out=WRTB[:, :], in_=wrt_d[:, :]).then_inc(S["s_gsm"], 16)
            g.dma_start(out=AB[:, :], in_=w2t_d[:, :]).then_inc(S["s_w2t"], 16)
            for sc in range(2):
                g.dma_start(out=SLA[:, D * sc:D * (sc + 1)],
                            in_=w1_d[sc, :, :]).then_inc(S[f"s_fw1_{sc % 2}"], 16)
            def xp_load(pp, gg):
                g.dma_start(
                    out=XP[pp][:, 4096 * gg:4096 * (gg + 1)].rearrange("p (c n) -> p c n", n=NT),
                    in_=xr[:, 8 * gg:8 * (gg + 1), NT * pp:NT * (pp + 1)],
                ).then_inc(xin_sem(pp, gg), 16)
            xp_load(0, 0)
            for sc in range(2, C):
                g.wait_ge(S["s_fmm"], sc - 1)
                g.dma_start(out=SLA[:, D * (sc % 2):D * (sc % 2 + 1)],
                            in_=w1_d[sc, :, :]).then_inc(S[f"s_fw1_{sc % 2}"], 16)
                if sc in (3, 5, 7):
                    xp_load(0, (sc - 1) // 2)
                if sc == 27:
                    g.wait_ge(S["s_fst_0"], 16 * 4 * 10)
                    g.wait_ge(S["s_fst_1"], 16 * 4 * 10)
                    g.collective_compute(
                        "AllGather", OP.bypass, replica_groups=[list(range(NCORE))],
                        ins=[asl_a.ap()], outs=[g1_d.ap()],
                    ).then_inc(S["s_cc"])
            g.wait_ge(S["s_fst_0"], 16 * 4 * (C // 2))
            g.wait_ge(S["s_fst_1"], 16 * 4 * (C // 2))
            g.collective_compute(
                "AllGather", OP.bypass, replica_groups=[list(range(NCORE))],
                ins=[asl_b.ap()], outs=[g2_d.ap()],
            ).then_inc(S["s_cc"])
            for gg in range(4):
                xp_load(1, gg)
            def out_store(p, c):
                g.wait_ge(S["s_app"], a4_cnt(p, c))
                g.dma_start(out=outr[:, c, NT * p:NT * (p + 1)],
                            in_=OUTS[:, NT * (c % 4):NT * (c % 4 + 1)]
                            ).then_inc(S[f"s_out_{c % 4}"], 16)
            for p in range(NP):
                for o in range(C):
                    k = C * p + o
                    if o == 2 and p >= 1:
                        for c in range(C):
                            out_store(p - 1, c)
                            if c % 8 == 7 and p + 1 < NP:
                                gg = c // 8
                                g.dma_start(
                                    out=XP[(p - 1) % 2][:, 4096 * gg:4096 * (gg + 1)].rearrange("p (c n) -> p c n", n=NT),
                                    in_=xr[:, 8 * gg:8 * (gg + 1), NT * (p + 1):NT * (p + 2)],
                                ).then_inc(xin_sem(p + 1, gg), 16)
                    if k >= 2:
                        p2, o2 = divmod(k - 2, C)
                        g.wait_ge(S["s_ymm"], 64 * p2 + 32 + o2 + 1)
                    if p == 0:
                        if k >= 2:
                            g.wait_ge(S["s_wbb"], 16 * (o - 1))
                        g.dma_start(out=SLW[:, D * (k % 2):D * (k % 2 + 1)],
                                    in_=wbt_d[o, :, :]).then_inc(S[f"s_slw_{k % 2}"], 16)
                        g.wait_ge(S[f"s_slw_{k % 2}"], 16 * (k // 2 + 1))
                        g.dma_start(out=wbb_d.ap()[o, :, :],
                                    in_=SLW[:, D * (k % 2):D * (k % 2 + 1)]
                                    ).then_inc(S["s_wbb"], 16)
                    else:
                        g.dma_start(out=SLW[:, D * (k % 2):D * (k % 2 + 1)],
                                    in_=wbb_d.ap()[o, :, :]).then_inc(S[f"s_slw_{k % 2}"], 16)
            for c in range(C):
                out_store(NP - 1, c)
            g.wait_ge(S["s_wbb"], 16 * C)
            for sl in range(4):
                g.wait_ge(S[f"s_out_{sl}"], 16 * 8 * NP)

        @block.sync
        def _(sy):
            sy.dma_start(out=B2C[:, :], in_=b2c_d[:, :]).then_inc(S["s_ssm"], 16)
            sy.dma_start(out=BBC[:, :], in_=bbc_d[:, :]).then_inc(S["s_ssm"], 16)
            sy.dma_start(out=E16B[:, :], in_=e16_d[:, :]).then_inc(S["s_ssm"], 16)
            sy.dma_start(out=IDB[:, :], in_=id_d[:, :]).then_inc(S["s_ssm"], 16)
            for sc in range(2):
                sy.dma_start(out=MK[sc], in_=msk_d[128 * sc:128 * (sc + 1), :]
                             ).then_inc(S[f"s_msk_{sc % 2}"], 16)
            for sc in range(C):
                sy.wait_ge(S["s_fev"], sc + 1)
                asl_h = asl_a if sc < 20 else asl_b
                scl = sc if sc < 20 else sc - 20
                for tl in range(4):
                    sy.dma_start(
                        out=asl_h.ap()[128 * tl:128 * (tl + 1), 128 * scl:128 * (scl + 1)],
                        in_=FEV[:, NT * (sc % 2) + 128 * tl:NT * (sc % 2) + 128 * (tl + 1)],
                    ).then_inc(S[f"s_fst_{sc % 2}"], 16)
                if sc + 2 < C:
                    sy.dma_start(out=MK[(sc + 2) % 2],
                                 in_=msk_d[128 * (sc + 2):128 * (sc + 3), :]
                                 ).then_inc(S[f"s_msk_{sc % 2}"], 16)
            sy.wait_ge(S["s_cc"], 1)
            g1r = g1_d.ap()
            g2r = g2_d.ap()
            for p in range(NP):
                for t in range(C):
                    k = C * p + t
                    if k >= 2:
                        p2, t2 = divmod(k - 2, C)
                        sy.wait_ge(S["s_ymm"], 64 * p2 + t2 + 1)
                    rblk, tl = divmod(t, 4)
                    row0 = rblk * 512 + tl * 128
                    sy.dma_start(out=SLA[:, D * (k % 2):D * (k % 2) + 128 * 20],
                                 in_=g1r[row0:row0 + 128, :]).then_inc(S[f"s_sla_{k % 2}"], 16)
                    if k == 0:
                        sy.wait_ge(S["s_cc"], 2)
                    sy.dma_start(out=SLA[:, D * (k % 2) + 128 * 20:D * (k % 2 + 1)],
                                 in_=g2r[row0:row0 + 128, :]).then_inc(S[f"s_sla_{k % 2}"], 16)
            sy.wait_ge(S["s_sla_0"], 2 * 16 * NP * C // 2)
            sy.wait_ge(S["s_sla_1"], 2 * 16 * NP * C // 2)

        @block.tensor
        def _(t):
            def ln1_stats(p, c):
                t.wait_ge(xin_sem(p, c // 8), 16 * (p // 2 + 1))
                if c == 0 and p > 0:
                    t.wait_ge(S["s_sm"], SM * (p - 1) + 13)
                t.matmul(PX[0:1, 0:NT], ONEC[:, 0:1], xpc(p, c),
                         start=(c == 0), stop=(c == C - 1))
                t.wait_ge(S["s_sqr1"], 32 * p + c + 1)
                t.matmul(PX[32:33, 0:NT], ONEC[:, 0:1], sq1s(c),
                         start=(c == 0), stop=(c == C - 1)).then_inc(S["s_psq1"], 1)

            def bcast(fin_v, mbc_v):
                t.wait_ge(S["s_fin"], fin_v)
                t.wait_ge(S["s_rstd"], fin_v)
                if mbc_v > 0:
                    t.wait_ge(S["s_mbc"], mbc_v)
                t.matmul(PM[:, 0:NT], ONER[0:1, :], mean, start=True, stop=True)
                t.matmul(PR[:, 0:NT], ONER[0:1, :], rstd, start=True, stop=True
                         ).then_inc(S["s_bc"], 1)

            t.wait_ge(S["s_w2t"], 16)
            t.wait_ge(S["s_ones"], 1)
            for sc in range(C):
                t.wait_ge(S[f"s_fw1_{sc % 2}"], 16 * (sc // 2 + 1))
                if sc >= 4:
                    t.wait_ge(S["s_fev"], sc - 3)
                last = None
                for dc in range(C):
                    last = t.matmul(PB[sc % 4][:, 0:NT], sla_t(sc % 2, dc), abch(dc),
                                    start=(dc == 0), stop=(dc == C - 1))
                last.then_inc(S["s_fmm"], 1)
                if sc >= 2:
                    ln1_stats(0, sc - 2)
            for c in range(C - 2, C):
                ln1_stats(0, c)
            t.wait_ge(S["s_ssm"], 64)
            t.wait_ge(S["s_gsm"], 48)
            bcast(1, 0)
            for p in range(NP):
                L = 4 * p

                def ln2_stats(c):
                    t.wait_ge(S["s_evac"], 64 * p + c + 1)
                    if c == 0 and L > 0:
                        t.wait_ge(S["s_fin"], L)
                    t.matmul(PH[0:1, 0:NT], ONEC[:, 0:1], abch(c),
                             start=(c == 0), stop=(c == C - 1))
                    t.wait_ge(S["s_sqr"], jsq(p, 2, c) + 1)
                    t.matmul(PH[32:33, 0:NT], ONEC[:, 0:1], sqs(jsq(p, 2, c)),
                             start=(c == 0), stop=(c == C - 1)).then_inc(S["s_psq"], 1)

                for tt in range(C):
                    k = C * p + tt
                    t.wait_ge(S[f"s_sla_{k % 2}"], 2 * 16 * (k // 2 + 1))
                    if 64 * p + tt - 3 > 0:
                        t.wait_ge(S["s_evac"], 64 * p + tt - 3)
                    last = None
                    for s in range(C):
                        if tt == 0:
                            t.wait_ge(S["s_app"], a1_cnt(p, s))
                        last = t.matmul(PB[tt % 4][:, 0:NT], sla_t(tt % 2, s), uch(s),
                                        start=(s == 0), stop=(s == C - 1))
                    last.then_inc(S["s_ymm"], 1)
                    if tt >= 1:
                        ln2_stats(tt - 1)
                ln2_stats(31)
                bcast(L + 2, L + 1)
                for c in range(C):
                    t.wait_ge(S["s_cast"], 32 * p + c + 1)
                    if c == 0:
                        t.wait_ge(S["s_fin"], L + 2)
                    t.matmul(PH[0:1, 0:NT], ONEC[:, 0:1], uch(c),
                             start=(c == 0), stop=(c == C - 1))
                    t.wait_ge(S["s_sqr"], jsq(p, 3, c) + 1)
                    t.matmul(PH[32:33, 0:NT], ONEC[:, 0:1], sqs(jsq(p, 3, c)),
                             start=(c == 0), stop=(c == C - 1)).then_inc(S["s_psq"], 1)
                bcast(L + 3, L + 2)
                if p > 0:
                    t.wait_ge(S["s_sm"], SM * p)
                    t.wait_ge(S["s_fin"], L + 1)
                last = None
                for c in range(C):
                    t.wait_ge(S["s_app"], a3_cnt(p, c))
                    last = t.matmul(PX[0:8, 0:NT], WRTB[:, 8 * c:8 * (c + 1)], uch(c),
                                    start=(c == 0), stop=(c == C - 1))
                last.then_inc(S["s_rlg"], 1)
                t.wait_ge(S["s_fin"], L + 3)
                if p > 0:
                    t.wait_ge(S["s_hwv"], p)
                last = None
                for c in range(C):
                    last = t.matmul(PH[:, 0:NT], ATB[:, 128 * c:128 * (c + 1)], uch(c),
                                    start=(c == 0), stop=(c == C - 1))
                last.then_inc(S["s_hmm"], 1)
                t.wait_ge(S["s_sm"], SM * p + 1)
                last = None
                for gg in range(4):
                    last = t.transpose(PX[:, 8 * gg:8 * (gg + 1)],
                                       lgsb[:, 128 * gg:128 * (gg + 1)], IDB[0:8, 0:8])
                last.then_inc(S["s_tp1"], 1)

                def f_base(o):
                    k = C * p + o
                    t.wait_ge(S[f"s_slw_{k % 2}"], 16 * (k // 2 + 1))
                    t.wait_ge(S["s_evac"], 64 * p + 29 + o)
                    for s in range(C):
                        t.matmul(PB[o % 4][:, 0:NT], slw_t(o % 2, s), uch(s),
                                 start=(s == 0), stop=False)

                def f_expert(o):
                    t.matmul(PB[o % 4][:, 0:NT], BTB[:, 128 * o:128 * (o + 1)], HW[:, :],
                             start=False, stop=True).then_inc(S["s_ymm"], 1)

                def ln4_stats(c):
                    t.wait_ge(S["s_evac"], 64 * p + 32 + c + 1)
                    if c == 0:
                        t.wait_ge(S["s_fin"], L + 3)
                        t.wait_ge(S["s_hwv"], p + 1)
                    t.matmul(PH[0:1, 0:NT], ONEC[:, 0:1], abch(c),
                             start=(c == 0), stop=(c == C - 1))
                    t.wait_ge(S["s_sqr"], jsq(p, 4, c) + 1)
                    t.matmul(PH[32:33, 0:NT], ONEC[:, 0:1], sqs(jsq(p, 4, c)),
                             start=(c == 0), stop=(c == C - 1)).then_inc(S["s_psq"], 1)
                f_base(0)
                f_base(1)
                t.wait_ge(S["s_sm"], SM * p + 11)
                last = None
                for gg in range(4):
                    last = t.transpose(PX[0:8, 128 * gg:128 * (gg + 1)],
                                       comb[:, 8 * gg:8 * (gg + 1)], IDB[:, :])
                last.then_inc(S["s_tp2"], 1)
                t.wait_ge(S["s_sm"], SM * p + 12)
                t.matmul(PX[:, 0:NT], E16B[:, :], cfm, start=True, stop=True
                         ).then_inc(S["s_cwb"], 1)
                t.wait_ge(S["s_hwv"], p + 1)
                f_expert(0)
                f_expert(1)
                ln4_stats(0)
                for o in range(2, C):
                    f_base(o)
                    f_expert(o)
                    ln4_stats(o - 1)
                    if p + 1 < NP:
                        ln1_stats(p + 1, o - 2)
                ln4_stats(31)
                if p + 1 < NP:
                    ln1_stats(p + 1, 30)
                    ln1_stats(p + 1, 31)
                bcast(L + 4, L + 3)
                if p + 1 < NP:
                    bcast(L + 5, L + 4)

        @block.scalar
        def _(s):
            for c in range(C):
                s.wait_ge(xin_sem(0, c // 8), 16)
                if c >= 2:
                    s.wait_ge(S["s_psq1"], c - 1)
                s.activation(sq1s(c), xpc(0, c), AF.Square).then_inc(S["s_sqr1"], 1)
            for p in range(NP):
                L = 4 * p
                if p == 0:
                    s.wait_ge(S["s_fin"], 1)
                    s.activation(var, var, AF.Ln)
                    s.activation(rstd, var, AF.Exp, scale=-0.5).then_inc(S["s_rstd"], 1)
                if p == 0:
                    s.wait_ge(S["s_fmm"], C)
                    s.wait_ge(S["s_ssm"], 64)
                for tt in range(C):
                    s.wait_ge(S["s_ymm"], 64 * p + tt + 1)
                    if p > 0:
                        s.wait_ge(S["s_app"], a4_cnt(p - 1, tt))
                    s.activation(abch(tt), PB[tt % 4][:, 0:NT], AF.Gelu,
                                 bias=B2C[:, tt:tt + 1]).then_inc(S["s_evac"], 1)
                    j = jsq(p, 2, tt)
                    if j - 2 >= 0:
                        s.wait_ge(S["s_psq"], j - 1)
                    s.activation(sqs(j), abch(tt), AF.Square).then_inc(S["s_sqr"], 1)
                s.wait_ge(S["s_fin"], L + 2)
                s.wait_ge(S["s_bc"], L + 1)
                s.activation(var, var, AF.Ln)
                s.activation(rstd, var, AF.Exp, scale=-0.5).then_inc(S["s_rstd"], 1)
                for c in range(C):
                    s.wait_ge(S["s_app"], a2_cnt(p, c))
                    j = jsq(p, 3, c)
                    s.wait_ge(S["s_psq"], j - 1)
                    s.activation(sqs(j), xpc(p, c), AF.Square).then_inc(S["s_sqr"], 1)
                s.wait_ge(S["s_fin"], L + 3)
                s.wait_ge(S["s_bc"], L + 2)
                s.activation(var, var, AF.Ln)
                s.activation(rstd, var, AF.Exp, scale=-0.5).then_inc(S["s_rstd"], 1)
                s.wait_ge(S["s_sm"], SM * p + 8)
                s.activation(elm, lm, AF.Exp)
                s.activation(e2v, M2, AF.Exp).then_inc(S["s_rexp"], 1)
                for o in range(C):
                    s.wait_ge(S["s_ymm"], 64 * p + 32 + o + 1)
                    s.activation(abch(o), PB[o % 4][:, 0:NT], AF.Identity,
                                 bias=BBC[:, o:o + 1]).then_inc(S["s_evac"], 1)
                    j = jsq(p, 4, o)
                    s.wait_ge(S["s_psq"], j - 1)
                    s.activation(sqs(j), abch(o), AF.Square).then_inc(S["s_sqr"], 1)
                    if p + 1 < NP:
                        c = o
                        s.wait_ge(xin_sem(p + 1, c // 8), 16 * ((p + 1) // 2 + 1))
                        s.wait_ge(S["s_psq1"], 32 * (p + 1) + c - 1)
                        s.activation(sq1s(c), xpc(p + 1, c), AF.Square).then_inc(S["s_sqr1"], 1)
                s.wait_ge(S["s_fin"], L + 4)
                s.wait_ge(S["s_bc"], L + 3)
                s.activation(var, var, AF.Ln)
                s.activation(rstd, var, AF.Exp, scale=-0.5).then_inc(S["s_rstd"], 1)
                if p + 1 < NP:
                    s.wait_ge(S["s_fin"], L + 5)
                    s.wait_ge(S["s_bc"], L + 4)
                    s.activation(var, var, AF.Ln)
                    s.activation(rstd, var, AF.Exp, scale=-0.5).then_inc(S["s_rstd"], 1)

        @block.vector
        def _(v):
            v.memset(ONEC[:, :], 1.0)
            v.memset(ONER[:, :], 1.0).then_inc(S["s_ones"], 1)
            for sc in range(C):
                v.wait_ge(S["s_fmm"], sc + 1)
                v.wait_ge(S[f"s_msk_{sc % 2}"], 16 * (sc // 2 + 1))
                if sc >= 2:
                    v.wait_ge(S[f"s_fst_{sc % 2}"], 16 * 4 * (sc // 2))
                v.tensor_tensor(FEV[:, NT * (sc % 2):NT * (sc % 2 + 1)],
                                PB[sc % 4][:, 0:NT], MK[sc % 2],
                                OP.mult).then_inc(S["s_fev"], 1)
            smc = [0]

            def sm_inc(ins):
                smc[0] += 1
                ins.then_inc(S["s_sm"], 1)

            def sm_step(ins):
                sm_inc(ins)
                v.wait_ge(S["s_sm"], smc[0])

            def finalize(bank, gl):
                if gl > 0:
                    v.wait_ge(S["s_bc"], gl)
                v.tensor_scalar(mean, bank[0:1, 0:NT], INV_D, None, OP.mult)
                v.tensor_scalar(var, bank[32:33, 0:NT], INV_D, EPS, OP.mult, OP.add)
                v.tensor_tensor(msq, mean, mean, OP.mult)
                v.tensor_tensor(var, var, msq, OP.subtract).then_inc(S["s_fin"], 1)

            def copies(mb, rb):
                v.tensor_copy(mb, PM[:, 0:NT])
                v.tensor_copy(rb, PR[:, 0:NT]).then_inc(S["s_mbc"], 1)

            def apply1_stream(p):
                for c in range(C):
                    v.wait_ge(xin_sem(p, c // 8), 16 * (p // 2 + 1))
                    v.tensor_tensor(T1[:, :], xpc(p, c), MB2, OP.subtract)
                    v.tensor_tensor(uch(c), T1[:, :], RB2, OP.mult).then_inc(S["s_app"], 1)

            v.wait_ge(S["s_psq1"], C)
            v.wait_ge(S["s_fst_0"], 16 * 4 * (C // 2))
            v.wait_ge(S["s_fst_1"], 16 * 4 * (C // 2))
            finalize(PX, 0)
            v.wait_ge(S["s_bc"], 1)
            copies(MB2, RB2)
            apply1_stream(0)
            for p in range(NP):
                L = 4 * p
                v.wait_ge(S["s_psq"], 96 * p + 32)
                finalize(PH, L + 1)
                v.wait_ge(S["s_bc"], L + 2)
                copies(MB[:, :], RB[:, :])
                for c in range(C):
                    v.tensor_tensor(T1[:, :], abch(c), MB[:, :], OP.subtract)
                    v.tensor_tensor(T1[:, :], T1[:, :], RB[:, :], OP.mult)
                    v.tensor_tensor(xpc(p, c), xpc(p, c), T1[:, :], OP.add).then_inc(S["s_app"], 1)
                    v.tensor_copy(uch(c), xpc(p, c)).then_inc(S["s_cast"], 1)
                v.wait_ge(S["s_psq"], 96 * p + 64)
                finalize(PH, L + 2)
                v.wait_ge(S["s_bc"], L + 3)
                copies(MB[:, :], RB[:, :])
                for c in range(C):
                    v.tensor_tensor(T1[:, :], uch(c), MB[:, :], OP.subtract)
                    v.tensor_tensor(uch(c), T1[:, :], RB[:, :], OP.mult).then_inc(S["s_app"], 1)
                v.wait_ge(S["s_rlg"], p + 1)
                sm_inc(v.tensor_copy(lgsb, PX[0:8, 0:NT]))
                v.wait_ge(S["s_tp1"], p + 1)
                sm_step(v.tensor_copy(lgtm, PX[:, 0:32]))
                sm_step(v.reduce_max(M1, lgtm.rearrange("p (g e) -> p g e", e=8), axis=AX.X))
                last = None
                for gg in range(4):
                    last = v.tensor_scalar(lm[:, 8 * gg:8 * (gg + 1)],
                                           lgtm[:, 8 * gg:8 * (gg + 1)],
                                           M1[:, gg:gg + 1], None, OP.subtract)
                sm_step(last)
                sm_step(v.tensor_scalar(mskS, lm, 0.0, None, OP.is_equal))
                sm_step(v.scalar_tensor_tensor(mskS, mskS, -1e30, lm, OP.mult, OP.add))
                sm_step(v.reduce_max(M2, mskS.rearrange("p (g e) -> p g e", e=8), axis=AX.X))
                last = None
                for gg in range(4):
                    last = v.tensor_scalar(sel[:, 8 * gg:8 * (gg + 1)],
                                           lm[:, 8 * gg:8 * (gg + 1)],
                                           M2[:, gg:gg + 1], None, OP.is_ge)
                sm_inc(last)
                v.wait_ge(S["s_rexp"], p + 1)
                sm_step(v.tensor_scalar(e2v, e2v, 0.5, 0.5, OP.mult, OP.add))
                sm_step(v.reciprocal(rec, e2v))
                last = None
                for gg in range(4):
                    last = v.scalar_tensor_tensor(comb[:, 8 * gg:8 * (gg + 1)],
                                                  elm[:, 8 * gg:8 * (gg + 1)],
                                                  rec[:, gg:gg + 1],
                                                  sel[:, 8 * gg:8 * (gg + 1)],
                                                  OP.mult, OP.mult)
                sm_inc(last)
                v.wait_ge(S["s_tp2"], p + 1)
                sm_inc(v.tensor_copy(cfm, PX[0:8, 0:NT]))
                v.wait_ge(S["s_cwb"], p + 1)
                sm_inc(v.tensor_copy(CW, PX[:, 0:NT]))
                v.wait_ge(S["s_hmm"], p + 1)
                v.tensor_tensor(HW[:, :], PH[:, 0:NT], CW, OP.mult).then_inc(S["s_hwv"], 1)
                v.wait_ge(S["s_psq"], 96 * p + 96)
                finalize(PH, L + 3)
                v.wait_ge(S["s_bc"], L + 4)
                copies(MB[:, :], RB[:, :])
                if p + 1 < NP:
                    v.wait_ge(S["s_psq1"], 32 * (p + 1) + 32)
                    finalize(PX, L + 4)
                    v.wait_ge(S["s_bc"], L + 5)
                    copies(MB2, RB2)
                    apply1_stream(p + 1)
                for c in range(C):
                    v.tensor_tensor(T1[:, :], abch(c), MB[:, :], OP.subtract)
                    v.tensor_tensor(T1[:, :], T1[:, :], RB[:, :], OP.mult)
                    if 8 * p + c // 4 > 0:
                        v.wait_ge(S[f"s_out_{c % 4}"], 16 * (8 * p + c // 4))
                    v.tensor_tensor(OUTS[:, NT * (c % 4):NT * (c % 4 + 1)],
                                    xpc(p, c), T1[:, :], OP.add).then_inc(S["s_app"], 1)
            assert smc[0] == SM * NP, smc[0]
    return nc


def kernel(L0_attn_x, L0_attn_w1, L0_attn_w2, L0_attn_b2, L0_attn_sparse_mask,
           L0_ffn_W_base, L0_ffn_b_base, L0_ffn_W_router, L0_ffn_A, L0_ffn_B):
    x = np.asarray(L0_attn_x, dtype=np.float32)
    w1_ = np.asarray(L0_attn_w1, dtype=np.float32)
    w1 = np.ascontiguousarray(w1_.reshape(C, 128, C, 128).transpose(2, 1, 0, 3).reshape(C, 128, D))
    w2t = np.ascontiguousarray(np.asarray(L0_attn_w2, dtype=np.float32).T)
    mask = np.ascontiguousarray(np.asarray(L0_attn_sparse_mask, dtype=np.float32))
    wbt_ = np.asarray(L0_ffn_W_base, dtype=np.float32).T
    wbt = np.ascontiguousarray(wbt_.reshape(C, 128, C, 128).transpose(2, 1, 0, 3).reshape(C, 128, D))
    b2c = np.ascontiguousarray(np.asarray(L0_attn_b2, dtype=np.float32).reshape(C, 128).T)
    bbc = np.ascontiguousarray(np.asarray(L0_ffn_b_base, dtype=np.float32).reshape(C, 128).T)
    A = np.asarray(L0_ffn_A, dtype=np.float32)
    B = np.asarray(L0_ffn_B, dtype=np.float32)
    Wr = np.asarray(L0_ffn_W_router, dtype=np.float32)
    at_r = np.ascontiguousarray(
        A.reshape(128, D).T.reshape(C, 128, 128).transpose(1, 0, 2).reshape(128, D))
    bt_r = np.ascontiguousarray(B.transpose(0, 2, 1).reshape(128, D))
    wrt_r = np.ascontiguousarray(
        Wr.T.reshape(C, 128, 8).transpose(1, 0, 2).reshape(128, 256))
    e16 = np.zeros((8, 128), dtype=np.float32)
    for e in range(8):
        e16[e, e * 16:(e + 1) * 16] = 1.0
    ident = np.eye(128, dtype=np.float32)

    nc = build_nc()
    in_maps = []
    for r in range(NCORE):
        xs = np.ascontiguousarray(x[0, r * TPC:(r + 1) * TPC, :].T)
        w2t_r = np.ascontiguousarray(
            w2t[:, r * 512:(r + 1) * 512].reshape(C, 128, 512).transpose(1, 0, 2).reshape(128, C * 512))
        in_maps.append({
            "x_fm": xs, "w1_r": w1, "w2t_r": w2t_r,
            "mask_slc": np.ascontiguousarray(mask[:, r * 512:(r + 1) * 512]),
            "wbt_r": wbt, "at_r": at_r, "bt_r": bt_r, "wrt_r": wrt_r,
            "b2c": b2c, "bbc": bbc, "e16": e16, "ident": ident,
        })
    res = run_bass_kernel_spmd(nc, in_maps, core_ids=list(range(NCORE)))
    out = np.empty((1, NCORE * TPC, D), dtype=np.float32)
    for r in range(NCORE):
        out[0, r * TPC:(r + 1) * TPC, :] = res.results[r]["out_fm"].T
    return out


# version 19
# speedup vs baseline: 1.0270x; 1.0270x over previous
import numpy as np
import concourse.bass as bass
import concourse.mybir as mybir
from concourse.bass_utils import run_bass_kernel_spmd

F32 = mybir.dt.float32
BF16 = mybir.dt.bfloat16
AF = mybir.ActivationFunctionType
OP = mybir.AluOpType
AX = mybir.AxisListType

D = 4096
C = 32
NCORE = 8
TPC = 2048
NT = 512
NP = TPC // NT
SM = 13
INV_D = 1.0 / D
EPS = 1e-5


def a1_cnt(p, c):
    return (c + 1) if p == 0 else 96 + 128 * (p - 1) + c + 1


def a2_cnt(p, c):
    return (32 if p == 0 else 128 * p + 32) + c + 1


def a3_cnt(p, c):
    return a2_cnt(p, c) + 32


def a4_cnt(p, c):
    return (128 * (p + 1) if p < 3 else 480) + c + 1


def jsq(p, k, c):
    return 96 * p + 32 * (k - 2) + c


def build_nc():
    nc = bass.Bass("TRN2", target_bir_lowering=False, debug=False)
    x_d = nc.dram_tensor("x_fm", [D, TPC], F32, kind="ExternalInput").ap()
    w1_d = nc.dram_tensor("w1_r", [C, 128, D], F32, kind="ExternalInput").ap()
    w2t_d = nc.dram_tensor("w2t_r", [128, C * NT], F32, kind="ExternalInput").ap()
    msk_d = nc.dram_tensor("mask_slc", [D, NT], F32, kind="ExternalInput").ap()
    wbt_d = nc.dram_tensor("wbt_r", [C, 128, D], F32, kind="ExternalInput").ap()
    at_d = nc.dram_tensor("at_r", [128, D], F32, kind="ExternalInput").ap()
    bt_d = nc.dram_tensor("bt_r", [128, D], F32, kind="ExternalInput").ap()
    wrt_d = nc.dram_tensor("wrt_r", [128, 256], F32, kind="ExternalInput").ap()
    b2c_d = nc.dram_tensor("b2c", [128, C], F32, kind="ExternalInput").ap()
    bbc_d = nc.dram_tensor("bbc", [128, C], F32, kind="ExternalInput").ap()
    e16_d = nc.dram_tensor("e16", [8, 128], F32, kind="ExternalInput").ap()
    id_d = nc.dram_tensor("ident", [128, 128], F32, kind="ExternalInput").ap()
    out_d = nc.dram_tensor("out_fm", [D, TPC], F32, kind="ExternalOutput").ap()
    SPL = 20
    asl_a = nc.dram_tensor("aslice_a", [4 * 128, 128 * SPL], BF16, kind="Internal")
    asl_b = nc.dram_tensor("aslice_b", [4 * 128, 128 * (C - SPL)], BF16, kind="Internal")
    g1_d = nc.dram_tensor("gath1", [NCORE * 4 * 128, 128 * SPL], BF16, kind="Internal", addr_space="Shared")
    g2_d = nc.dram_tensor("gath2", [NCORE * 4 * 128, 128 * (C - SPL)], BF16, kind="Internal", addr_space="Shared")
    wbb_d = nc.dram_tensor("wbb", [C, 128, D], BF16, kind="Internal")

    from contextlib import ExitStack
    with ExitStack() as ctx:
        E = ctx.enter_context
        XP = [E(nc.sbuf_tensor(f"XP{i}", [128, C * NT], BF16)) for i in range(2)]
        U = E(nc.sbuf_tensor("U", [128, C * NT], BF16))
        AB = E(nc.sbuf_tensor("AB", [128, C * NT], BF16))
        SLA = E(nc.sbuf_tensor("SLA", [128, 2 * D], BF16))
        SLW = E(nc.sbuf_tensor("SLW", [128, 2 * D], BF16))
        SQ = E(nc.sbuf_tensor("SQ", [128, 2 * NT], BF16))
        SQ1 = E(nc.sbuf_tensor("SQ1", [128, 2 * NT], BF16))
        T1 = E(nc.sbuf_tensor("T1", [128, NT], BF16))
        OUTS = E(nc.sbuf_tensor("OUTS", [128, 4 * NT], F32))
        FEV = E(nc.sbuf_tensor("FEV", [128, 2 * NT], BF16))
        HW = E(nc.sbuf_tensor("HW", [128, NT], BF16))
        ARN = E(nc.sbuf_tensor("ARN", [128, 2048], F32))
        ATB = E(nc.sbuf_tensor("ATB", [128, D], BF16))
        BTB = E(nc.sbuf_tensor("BTB", [128, D], BF16))
        WRTB = E(nc.sbuf_tensor("WRTB", [128, 256], BF16))
        B2C = E(nc.sbuf_tensor("B2C", [128, C], F32))
        BBC = E(nc.sbuf_tensor("BBC", [128, C], F32))
        E16B = E(nc.sbuf_tensor("E16B", [8, 128], F32))
        IDB = E(nc.sbuf_tensor("IDB", [128, 128], F32))
        ONEC = E(nc.sbuf_tensor("ONEC", [128, 1], BF16))
        ONER = E(nc.sbuf_tensor("ONER", [1, 128], F32))
        MB = E(nc.sbuf_tensor("MB", [128, NT], BF16))
        RB = E(nc.sbuf_tensor("RB", [128, NT], BF16))
        PB = [E(nc.psum_tensor(f"PB{i}", [128, 512], F32)) for i in range(4)]
        PM = E(nc.psum_tensor("PM", [128, 512], F32))
        PR = E(nc.psum_tensor("PR", [128, 512], F32))
        PH = E(nc.psum_tensor("PH", [128, 512], F32))
        PX = E(nc.psum_tensor("PX", [128, 512], F32))
        sem_names = [
            "s_gsm", "s_ssm", "s_w2t", "s_fw1_0", "s_fw1_1", "s_fmm", "s_fev",
            "s_fst_0", "s_fst_1", "s_msk_0", "s_msk_1", "s_cc",
            "s_xin_0", "s_xin_1", "s_xin_2", "s_xin_3",
            "s_xin_4", "s_xin_5", "s_xin_6", "s_xin_7",
            "s_out_0", "s_out_1", "s_out_2", "s_out_3",
            "s_sla_0", "s_sla_1", "s_slw_0", "s_slw_1", "s_wbb", "s_ones",
            "s_cast", "s_sqr", "s_psq", "s_sqr1", "s_psq1",
            "s_fin", "s_rstd", "s_bc", "s_app", "s_mbc",
            "s_ymm", "s_evac", "s_rlg", "s_hmm", "s_hwv", "s_cwb",
            "s_tp1", "s_tp2", "s_rexp", "s_sm",
        ]
        S = {n: E(nc.semaphore(n)) for n in sem_names}
        block = E(nc.Block())

        mean = ARN[0:1, 0:512]
        var = ARN[0:1, 512:1024]
        msq = ARN[0:1, 1024:1536]
        rstd = ARN[0:1, 1536:2048]
        lgsb = ARN[0:8, 0:512]
        cfm = ARN[0:8, 512:1024]
        CW = ARN[:, 1024:1536]
        lgtm = ARN[:, 1536:1568]
        M1 = ARN[:, 1568:1572]
        M2 = ARN[:, 1572:1576]
        e2v = ARN[:, 1576:1580]
        rec = ARN[:, 1580:1584]
        lm = ARN[:, 1584:1616]
        mskS = ARN[:, 1616:1648]
        sel = ARN[:, 1648:1680]
        elm = ARN[:, 1680:1712]
        comb = ARN[:, 1712:1744]
        MB2 = FEV[:, 0:NT]
        RB2 = FEV[:, NT:2 * NT]
        MK = [OUTS[:, 0:NT], OUTS[:, NT:2 * NT]]

        def uch(c):
            return U[:, NT * c:NT * (c + 1)]

        def xpc(p, c):
            return XP[p % 2][:, NT * c:NT * (c + 1)]

        def abch(c):
            return AB[:, NT * c:NT * (c + 1)]

        def sla_t(b, s):
            return SLA[:, D * b + 128 * s: D * b + 128 * s + 128]

        def slw_t(b, s):
            return SLW[:, D * b + 128 * s: D * b + 128 * s + 128]

        def sqs(j):
            return SQ[:, NT * (j % 2): NT * (j % 2) + NT]

        def sq1s(c):
            return SQ1[:, NT * (c % 2): NT * (c % 2) + NT]

        def xin_sem(p, g):
            return S[f"s_xin_{(p % 2) * 4 + g}"]

        xr = x_d.rearrange("(c p) n -> p c n", p=128)
        outr = out_d.rearrange("(c p) n -> p c n", p=128)

        @block.gpsimd
        def _(g):
            g.dma_start(out=ATB[:, :], in_=at_d[:, :]).then_inc(S["s_gsm"], 16)
            g.dma_start(out=BTB[:, :], in_=bt_d[:, :]).then_inc(S["s_gsm"], 16)
            g.dma_start(out=WRTB[:, :], in_=wrt_d[:, :]).then_inc(S["s_gsm"], 16)
            g.dma_start(out=AB[:, :], in_=w2t_d[:, :]).then_inc(S["s_w2t"], 16)
            for sc in range(2):
                g.dma_start(out=SLA[:, D * sc:D * (sc + 1)],
                            in_=w1_d[sc, :, :]).then_inc(S[f"s_fw1_{sc % 2}"], 16)
            def xp_load(pp, gg):
                g.dma_start(
                    out=XP[pp][:, 4096 * gg:4096 * (gg + 1)].rearrange("p (c n) -> p c n", n=NT),
                    in_=xr[:, 8 * gg:8 * (gg + 1), NT * pp:NT * (pp + 1)],
                ).then_inc(xin_sem(pp, gg), 16)
            xp_load(0, 0)
            for sc in range(2, C):
                g.wait_ge(S["s_fmm"], sc - 1)
                g.dma_start(out=SLA[:, D * (sc % 2):D * (sc % 2 + 1)],
                            in_=w1_d[sc, :, :]).then_inc(S[f"s_fw1_{sc % 2}"], 16)
                if sc in (3, 5, 7):
                    xp_load(0, (sc - 1) // 2)
                if sc == 27:
                    g.wait_ge(S["s_fst_0"], 16 * 4 * 10)
                    g.wait_ge(S["s_fst_1"], 16 * 4 * 10)
                    g.collective_compute(
                        "AllGather", OP.bypass, replica_groups=[list(range(NCORE))],
                        ins=[asl_a.ap()], outs=[g1_d.ap()],
                    ).then_inc(S["s_cc"])
            g.wait_ge(S["s_fst_0"], 16 * 4 * (C // 2))
            g.wait_ge(S["s_fst_1"], 16 * 4 * (C // 2))
            g.collective_compute(
                "AllGather", OP.bypass, replica_groups=[list(range(NCORE))],
                ins=[asl_b.ap()], outs=[g2_d.ap()],
            ).then_inc(S["s_cc"])
            for gg in range(4):
                xp_load(1, gg)
            def out_store(p, c):
                g.wait_ge(S["s_app"], a4_cnt(p, c))
                g.dma_start(out=outr[:, c, NT * p:NT * (p + 1)],
                            in_=OUTS[:, NT * (c % 4):NT * (c % 4 + 1)]
                            ).then_inc(S[f"s_out_{c % 4}"], 16)
            for p in range(NP):
                for o in range(C):
                    k = C * p + o
                    if o == 2 and p >= 1:
                        for c in range(C):
                            out_store(p - 1, c)
                            if c % 8 == 7 and p + 1 < NP:
                                gg = c // 8
                                g.dma_start(
                                    out=XP[(p - 1) % 2][:, 4096 * gg:4096 * (gg + 1)].rearrange("p (c n) -> p c n", n=NT),
                                    in_=xr[:, 8 * gg:8 * (gg + 1), NT * (p + 1):NT * (p + 2)],
                                ).then_inc(xin_sem(p + 1, gg), 16)
                    if k >= 2:
                        p2, o2 = divmod(k - 2, C)
                        g.wait_ge(S["s_ymm"], 64 * p2 + 32 + o2 + 1)
                    if p == 0:
                        if k >= 2:
                            g.wait_ge(S["s_wbb"], 16 * (o - 1))
                        g.dma_start(out=SLW[:, D * (k % 2):D * (k % 2 + 1)],
                                    in_=wbt_d[o, :, :]).then_inc(S[f"s_slw_{k % 2}"], 16)
                        g.wait_ge(S[f"s_slw_{k % 2}"], 16 * (k // 2 + 1))
                        g.dma_start(out=wbb_d.ap()[o, :, :],
                                    in_=SLW[:, D * (k % 2):D * (k % 2 + 1)]
                                    ).then_inc(S["s_wbb"], 16)
                    else:
                        g.dma_start(out=SLW[:, D * (k % 2):D * (k % 2 + 1)],
                                    in_=wbb_d.ap()[o, :, :]).then_inc(S[f"s_slw_{k % 2}"], 16)
            for c in range(C):
                out_store(NP - 1, c)
            g.wait_ge(S["s_wbb"], 16 * C)
            for sl in range(4):
                g.wait_ge(S[f"s_out_{sl}"], 16 * 8 * NP)

        @block.sync
        def _(sy):
            sy.dma_start(out=B2C[:, :], in_=b2c_d[:, :]).then_inc(S["s_ssm"], 16)
            sy.dma_start(out=BBC[:, :], in_=bbc_d[:, :]).then_inc(S["s_ssm"], 16)
            sy.dma_start(out=E16B[:, :], in_=e16_d[:, :]).then_inc(S["s_ssm"], 16)
            sy.dma_start(out=IDB[:, :], in_=id_d[:, :]).then_inc(S["s_ssm"], 16)
            for sc in range(2):
                sy.dma_start(out=MK[sc], in_=msk_d[128 * sc:128 * (sc + 1), :]
                             ).then_inc(S[f"s_msk_{sc % 2}"], 16)
            for sc in range(C):
                sy.wait_ge(S["s_fev"], sc + 1)
                asl_h = asl_a if sc < 20 else asl_b
                scl = sc if sc < 20 else sc - 20
                for tl in range(4):
                    sy.dma_start(
                        out=asl_h.ap()[128 * tl:128 * (tl + 1), 128 * scl:128 * (scl + 1)],
                        in_=FEV[:, NT * (sc % 2) + 128 * tl:NT * (sc % 2) + 128 * (tl + 1)],
                    ).then_inc(S[f"s_fst_{sc % 2}"], 16)
                if sc + 2 < C:
                    sy.dma_start(out=MK[(sc + 2) % 2],
                                 in_=msk_d[128 * (sc + 2):128 * (sc + 3), :]
                                 ).then_inc(S[f"s_msk_{sc % 2}"], 16)
            sy.wait_ge(S["s_cc"], 1)
            g1r = g1_d.ap()
            g2r = g2_d.ap()
            for p in range(NP):
                for t in range(C):
                    k = C * p + t
                    if k >= 2:
                        p2, t2 = divmod(k - 2, C)
                        sy.wait_ge(S["s_ymm"], 64 * p2 + t2 + 1)
                    rblk, tl = divmod(t, 4)
                    row0 = rblk * 512 + tl * 128
                    sy.dma_start(out=SLA[:, D * (k % 2):D * (k % 2) + 128 * 20],
                                 in_=g1r[row0:row0 + 128, :]).then_inc(S[f"s_sla_{k % 2}"], 16)
                    if k == 0:
                        sy.wait_ge(S["s_cc"], 2)
                    sy.dma_start(out=SLA[:, D * (k % 2) + 128 * 20:D * (k % 2 + 1)],
                                 in_=g2r[row0:row0 + 128, :]).then_inc(S[f"s_sla_{k % 2}"], 16)
            sy.wait_ge(S["s_sla_0"], 2 * 16 * NP * C // 2)
            sy.wait_ge(S["s_sla_1"], 2 * 16 * NP * C // 2)

        @block.tensor
        def _(t):
            def ln1_stats(p, c):
                t.wait_ge(xin_sem(p, c // 8), 16 * (p // 2 + 1))
                if c == 0 and p > 0:
                    t.wait_ge(S["s_sm"], SM * (p - 1) + 13)
                t.matmul(PX[0:1, 0:NT], ONEC[:, 0:1], xpc(p, c),
                         start=(c == 0), stop=(c == C - 1))
                t.wait_ge(S["s_sqr1"], 32 * p + c + 1)
                t.matmul(PX[32:33, 0:NT], ONEC[:, 0:1], sq1s(c),
                         start=(c == 0), stop=(c == C - 1)).then_inc(S["s_psq1"], 1)

            def bcast(fin_v, mbc_v):
                t.wait_ge(S["s_fin"], fin_v)
                t.wait_ge(S["s_rstd"], fin_v)
                if mbc_v > 0:
                    t.wait_ge(S["s_mbc"], mbc_v)
                t.matmul(PM[:, 0:NT], ONER[0:1, :], mean, start=True, stop=True)
                t.matmul(PR[:, 0:NT], ONER[0:1, :], rstd, start=True, stop=True
                         ).then_inc(S["s_bc"], 1)

            t.wait_ge(S["s_w2t"], 16)
            t.wait_ge(S["s_ones"], 1)
            for sc in range(C):
                t.wait_ge(S[f"s_fw1_{sc % 2}"], 16 * (sc // 2 + 1))
                if sc >= 4:
                    t.wait_ge(S["s_fev"], sc - 3)
                last = None
                for dc in range(C):
                    last = t.matmul(PB[sc % 4][:, 0:NT], sla_t(sc % 2, dc), abch(dc),
                                    start=(dc == 0), stop=(dc == C - 1))
                last.then_inc(S["s_fmm"], 1)
                if sc >= 2:
                    ln1_stats(0, sc - 2)
            for c in range(C - 2, C):
                ln1_stats(0, c)
            t.wait_ge(S["s_ssm"], 64)
            t.wait_ge(S["s_gsm"], 48)
            bcast(1, 0)
            for p in range(NP):
                L = 4 * p

                def ln2_stats(c):
                    t.wait_ge(S["s_evac"], 64 * p + c + 1)
                    if c == 0 and L > 0:
                        t.wait_ge(S["s_fin"], L)
                    t.matmul(PH[0:1, 0:NT], ONEC[:, 0:1], abch(c),
                             start=(c == 0), stop=(c == C - 1))
                    t.wait_ge(S["s_sqr"], jsq(p, 2, c) + 1)
                    t.matmul(PH[32:33, 0:NT], ONEC[:, 0:1], sqs(jsq(p, 2, c)),
                             start=(c == 0), stop=(c == C - 1)).then_inc(S["s_psq"], 1)

                for tt in range(C):
                    k = C * p + tt
                    t.wait_ge(S[f"s_sla_{k % 2}"], 2 * 16 * (k // 2 + 1))
                    if 64 * p + tt - 3 > 0:
                        t.wait_ge(S["s_evac"], 64 * p + tt - 3)
                    last = None
                    for s in range(C):
                        if tt == 0:
                            t.wait_ge(S["s_app"], a1_cnt(p, s))
                        last = t.matmul(PB[tt % 4][:, 0:NT], sla_t(tt % 2, s), uch(s),
                                        start=(s == 0), stop=(s == C - 1))
                    last.then_inc(S["s_ymm"], 1)
                    if tt >= 1:
                        ln2_stats(tt - 1)
                ln2_stats(31)
                bcast(L + 2, L + 1)
                for c in range(C):
                    t.wait_ge(S["s_app"], a2_cnt(p, c))
                    if c == 0:
                        t.wait_ge(S["s_fin"], L + 2)
                    t.matmul(PH[0:1, 0:NT], ONEC[:, 0:1], xpc(p, c),
                             start=(c == 0), stop=(c == C - 1))
                    t.wait_ge(S["s_sqr"], jsq(p, 3, c) + 1)
                    t.matmul(PH[32:33, 0:NT], ONEC[:, 0:1], sqs(jsq(p, 3, c)),
                             start=(c == 0), stop=(c == C - 1)).then_inc(S["s_psq"], 1)
                bcast(L + 3, L + 2)
                if p > 0:
                    t.wait_ge(S["s_sm"], SM * p)
                    t.wait_ge(S["s_fin"], L + 1)
                last = None
                for c in range(C):
                    t.wait_ge(S["s_app"], a3_cnt(p, c))
                    last = t.matmul(PX[0:8, 0:NT], WRTB[:, 8 * c:8 * (c + 1)], uch(c),
                                    start=(c == 0), stop=(c == C - 1))
                last.then_inc(S["s_rlg"], 1)
                t.wait_ge(S["s_fin"], L + 3)
                if p > 0:
                    t.wait_ge(S["s_hwv"], p)
                last = None
                for c in range(C):
                    last = t.matmul(PH[:, 0:NT], ATB[:, 128 * c:128 * (c + 1)], uch(c),
                                    start=(c == 0), stop=(c == C - 1))
                last.then_inc(S["s_hmm"], 1)
                t.wait_ge(S["s_sm"], SM * p + 1)
                last = None
                for gg in range(4):
                    last = t.transpose(PX[:, 8 * gg:8 * (gg + 1)],
                                       lgsb[:, 128 * gg:128 * (gg + 1)], IDB[0:8, 0:8])
                last.then_inc(S["s_tp1"], 1)

                def f_base(o):
                    k = C * p + o
                    t.wait_ge(S[f"s_slw_{k % 2}"], 16 * (k // 2 + 1))
                    t.wait_ge(S["s_evac"], 64 * p + 29 + o)
                    for s in range(C):
                        t.matmul(PB[o % 4][:, 0:NT], slw_t(o % 2, s), uch(s),
                                 start=(s == 0), stop=False)

                def f_expert(o):
                    t.matmul(PB[o % 4][:, 0:NT], BTB[:, 128 * o:128 * (o + 1)], HW[:, :],
                             start=False, stop=True).then_inc(S["s_ymm"], 1)

                def ln4_stats(c):
                    t.wait_ge(S["s_evac"], 64 * p + 32 + c + 1)
                    if c == 0:
                        t.wait_ge(S["s_fin"], L + 3)
                        t.wait_ge(S["s_hwv"], p + 1)
                    t.matmul(PH[0:1, 0:NT], ONEC[:, 0:1], abch(c),
                             start=(c == 0), stop=(c == C - 1))
                    t.wait_ge(S["s_sqr"], jsq(p, 4, c) + 1)
                    t.matmul(PH[32:33, 0:NT], ONEC[:, 0:1], sqs(jsq(p, 4, c)),
                             start=(c == 0), stop=(c == C - 1)).then_inc(S["s_psq"], 1)
                f_base(0)
                f_base(1)
                t.wait_ge(S["s_sm"], SM * p + 11)
                last = None
                for gg in range(4):
                    last = t.transpose(PX[0:8, 128 * gg:128 * (gg + 1)],
                                       comb[:, 8 * gg:8 * (gg + 1)], IDB[:, :])
                last.then_inc(S["s_tp2"], 1)
                t.wait_ge(S["s_sm"], SM * p + 12)
                t.matmul(PX[:, 0:NT], E16B[:, :], cfm, start=True, stop=True
                         ).then_inc(S["s_cwb"], 1)
                t.wait_ge(S["s_hwv"], p + 1)
                f_expert(0)
                f_expert(1)
                ln4_stats(0)
                for o in range(2, C):
                    f_base(o)
                    f_expert(o)
                    ln4_stats(o - 1)
                    if p + 1 < NP:
                        ln1_stats(p + 1, o - 2)
                ln4_stats(31)
                if p + 1 < NP:
                    ln1_stats(p + 1, 30)
                    ln1_stats(p + 1, 31)
                bcast(L + 4, L + 3)
                if p + 1 < NP:
                    bcast(L + 5, L + 4)

        @block.scalar
        def _(s):
            for c in range(C):
                s.wait_ge(xin_sem(0, c // 8), 16)
                if c >= 2:
                    s.wait_ge(S["s_psq1"], c - 1)
                s.activation(sq1s(c), xpc(0, c), AF.Square).then_inc(S["s_sqr1"], 1)
            for p in range(NP):
                L = 4 * p
                if p == 0:
                    s.wait_ge(S["s_fin"], 1)
                    s.activation(var, var, AF.Ln)
                    s.activation(rstd, var, AF.Exp, scale=-0.5).then_inc(S["s_rstd"], 1)
                if p == 0:
                    s.wait_ge(S["s_fmm"], C)
                    s.wait_ge(S["s_ssm"], 64)
                for tt in range(C):
                    s.wait_ge(S["s_ymm"], 64 * p + tt + 1)
                    if p > 0:
                        s.wait_ge(S["s_app"], a4_cnt(p - 1, tt))
                    s.activation(abch(tt), PB[tt % 4][:, 0:NT], AF.Gelu,
                                 bias=B2C[:, tt:tt + 1]).then_inc(S["s_evac"], 1)
                    j = jsq(p, 2, tt)
                    if j - 2 >= 0:
                        s.wait_ge(S["s_psq"], j - 1)
                    s.activation(sqs(j), abch(tt), AF.Square).then_inc(S["s_sqr"], 1)
                s.wait_ge(S["s_fin"], L + 2)
                s.wait_ge(S["s_bc"], L + 1)
                s.activation(var, var, AF.Ln)
                s.activation(rstd, var, AF.Exp, scale=-0.5).then_inc(S["s_rstd"], 1)
                for c in range(C):
                    s.wait_ge(S["s_app"], a2_cnt(p, c))
                    j = jsq(p, 3, c)
                    s.wait_ge(S["s_psq"], j - 1)
                    s.activation(sqs(j), xpc(p, c), AF.Square).then_inc(S["s_sqr"], 1)
                s.wait_ge(S["s_fin"], L + 3)
                s.wait_ge(S["s_bc"], L + 2)
                s.activation(var, var, AF.Ln)
                s.activation(rstd, var, AF.Exp, scale=-0.5).then_inc(S["s_rstd"], 1)
                s.wait_ge(S["s_sm"], SM * p + 8)
                s.activation(elm, lm, AF.Exp)
                s.activation(e2v, M2, AF.Exp).then_inc(S["s_rexp"], 1)
                for o in range(C):
                    s.wait_ge(S["s_ymm"], 64 * p + 32 + o + 1)
                    s.activation(abch(o), PB[o % 4][:, 0:NT], AF.Identity,
                                 bias=BBC[:, o:o + 1]).then_inc(S["s_evac"], 1)
                    j = jsq(p, 4, o)
                    s.wait_ge(S["s_psq"], j - 1)
                    s.activation(sqs(j), abch(o), AF.Square).then_inc(S["s_sqr"], 1)
                    if p + 1 < NP:
                        c = o
                        s.wait_ge(xin_sem(p + 1, c // 8), 16 * ((p + 1) // 2 + 1))
                        s.wait_ge(S["s_psq1"], 32 * (p + 1) + c - 1)
                        s.activation(sq1s(c), xpc(p + 1, c), AF.Square).then_inc(S["s_sqr1"], 1)
                s.wait_ge(S["s_fin"], L + 4)
                s.wait_ge(S["s_bc"], L + 3)
                s.activation(var, var, AF.Ln)
                s.activation(rstd, var, AF.Exp, scale=-0.5).then_inc(S["s_rstd"], 1)
                if p + 1 < NP:
                    s.wait_ge(S["s_fin"], L + 5)
                    s.wait_ge(S["s_bc"], L + 4)
                    s.activation(var, var, AF.Ln)
                    s.activation(rstd, var, AF.Exp, scale=-0.5).then_inc(S["s_rstd"], 1)

        @block.vector
        def _(v):
            v.memset(ONEC[:, :], 1.0)
            v.memset(ONER[:, :], 1.0).then_inc(S["s_ones"], 1)
            for sc in range(C):
                v.wait_ge(S["s_fmm"], sc + 1)
                v.wait_ge(S[f"s_msk_{sc % 2}"], 16 * (sc // 2 + 1))
                if sc >= 2:
                    v.wait_ge(S[f"s_fst_{sc % 2}"], 16 * 4 * (sc // 2))
                v.tensor_tensor(FEV[:, NT * (sc % 2):NT * (sc % 2 + 1)],
                                PB[sc % 4][:, 0:NT], MK[sc % 2],
                                OP.mult).then_inc(S["s_fev"], 1)
            smc = [0]

            def sm_inc(ins):
                smc[0] += 1
                ins.then_inc(S["s_sm"], 1)

            def sm_step(ins):
                sm_inc(ins)
                v.wait_ge(S["s_sm"], smc[0])

            def finalize(bank, gl):
                if gl > 0:
                    v.wait_ge(S["s_bc"], gl)
                v.tensor_scalar(mean, bank[0:1, 0:NT], INV_D, None, OP.mult)
                v.tensor_scalar(var, bank[32:33, 0:NT], INV_D, EPS, OP.mult, OP.add)
                v.tensor_tensor(msq, mean, mean, OP.mult)
                v.tensor_tensor(var, var, msq, OP.subtract).then_inc(S["s_fin"], 1)

            def copies(mb, rb):
                v.tensor_copy(mb, PM[:, 0:NT])
                v.tensor_copy(rb, PR[:, 0:NT]).then_inc(S["s_mbc"], 1)

            def apply1_stream(p):
                for c in range(C):
                    v.wait_ge(xin_sem(p, c // 8), 16 * (p // 2 + 1))
                    v.tensor_tensor(T1[:, :], xpc(p, c), MB2, OP.subtract)
                    v.tensor_tensor(uch(c), T1[:, :], RB2, OP.mult).then_inc(S["s_app"], 1)

            v.wait_ge(S["s_psq1"], C)
            v.wait_ge(S["s_fst_0"], 16 * 4 * (C // 2))
            v.wait_ge(S["s_fst_1"], 16 * 4 * (C // 2))
            finalize(PX, 0)
            v.wait_ge(S["s_bc"], 1)
            copies(MB2, RB2)
            apply1_stream(0)
            for p in range(NP):
                L = 4 * p
                v.wait_ge(S["s_psq"], 96 * p + 32)
                finalize(PH, L + 1)
                v.wait_ge(S["s_bc"], L + 2)
                copies(MB[:, :], RB[:, :])
                for c in range(C):
                    v.tensor_tensor(T1[:, :], abch(c), MB[:, :], OP.subtract)
                    v.tensor_tensor(T1[:, :], T1[:, :], RB[:, :], OP.mult)
                    v.tensor_tensor(xpc(p, c), xpc(p, c), T1[:, :], OP.add).then_inc(S["s_app"], 1)
                    v.tensor_copy(uch(c), xpc(p, c)).then_inc(S["s_cast"], 1)
                v.wait_ge(S["s_psq"], 96 * p + 64)
                finalize(PH, L + 2)
                v.wait_ge(S["s_bc"], L + 3)
                copies(MB[:, :], RB[:, :])
                for c in range(C):
                    v.tensor_tensor(T1[:, :], xpc(p, c), MB[:, :], OP.subtract)
                    v.tensor_tensor(uch(c), T1[:, :], RB[:, :], OP.mult).then_inc(S["s_app"], 1)
                v.wait_ge(S["s_rlg"], p + 1)
                sm_inc(v.tensor_copy(lgsb, PX[0:8, 0:NT]))
                v.wait_ge(S["s_tp1"], p + 1)
                sm_step(v.tensor_copy(lgtm, PX[:, 0:32]))
                sm_step(v.reduce_max(M1, lgtm.rearrange("p (g e) -> p g e", e=8), axis=AX.X))
                last = None
                for gg in range(4):
                    last = v.tensor_scalar(lm[:, 8 * gg:8 * (gg + 1)],
                                           lgtm[:, 8 * gg:8 * (gg + 1)],
                                           M1[:, gg:gg + 1], None, OP.subtract)
                sm_step(last)
                sm_step(v.tensor_scalar(mskS, lm, 0.0, None, OP.is_equal))
                sm_step(v.scalar_tensor_tensor(mskS, mskS, -1e30, lm, OP.mult, OP.add))
                sm_step(v.reduce_max(M2, mskS.rearrange("p (g e) -> p g e", e=8), axis=AX.X))
                last = None
                for gg in range(4):
                    last = v.tensor_scalar(sel[:, 8 * gg:8 * (gg + 1)],
                                           lm[:, 8 * gg:8 * (gg + 1)],
                                           M2[:, gg:gg + 1], None, OP.is_ge)
                sm_inc(last)
                v.wait_ge(S["s_rexp"], p + 1)
                sm_step(v.tensor_scalar(e2v, e2v, 0.5, 0.5, OP.mult, OP.add))
                sm_step(v.reciprocal(rec, e2v))
                last = None
                for gg in range(4):
                    last = v.scalar_tensor_tensor(comb[:, 8 * gg:8 * (gg + 1)],
                                                  elm[:, 8 * gg:8 * (gg + 1)],
                                                  rec[:, gg:gg + 1],
                                                  sel[:, 8 * gg:8 * (gg + 1)],
                                                  OP.mult, OP.mult)
                sm_inc(last)
                v.wait_ge(S["s_tp2"], p + 1)
                sm_inc(v.tensor_copy(cfm, PX[0:8, 0:NT]))
                v.wait_ge(S["s_cwb"], p + 1)
                sm_inc(v.tensor_copy(CW, PX[:, 0:NT]))
                v.wait_ge(S["s_hmm"], p + 1)
                v.tensor_tensor(HW[:, :], PH[:, 0:NT], CW, OP.mult).then_inc(S["s_hwv"], 1)
                v.wait_ge(S["s_psq"], 96 * p + 96)
                finalize(PH, L + 3)
                v.wait_ge(S["s_bc"], L + 4)
                copies(MB[:, :], RB[:, :])
                if p + 1 < NP:
                    v.wait_ge(S["s_psq1"], 32 * (p + 1) + 32)
                    finalize(PX, L + 4)
                    v.wait_ge(S["s_bc"], L + 5)
                    copies(MB2, RB2)
                    apply1_stream(p + 1)
                for c in range(C):
                    v.tensor_tensor(T1[:, :], abch(c), MB[:, :], OP.subtract)
                    v.tensor_tensor(T1[:, :], T1[:, :], RB[:, :], OP.mult)
                    if 8 * p + c // 4 > 0:
                        v.wait_ge(S[f"s_out_{c % 4}"], 16 * (8 * p + c // 4))
                    v.tensor_tensor(OUTS[:, NT * (c % 4):NT * (c % 4 + 1)],
                                    xpc(p, c), T1[:, :], OP.add).then_inc(S["s_app"], 1)
            assert smc[0] == SM * NP, smc[0]
    return nc


def kernel(L0_attn_x, L0_attn_w1, L0_attn_w2, L0_attn_b2, L0_attn_sparse_mask,
           L0_ffn_W_base, L0_ffn_b_base, L0_ffn_W_router, L0_ffn_A, L0_ffn_B):
    x = np.asarray(L0_attn_x, dtype=np.float32)
    w1_ = np.asarray(L0_attn_w1, dtype=np.float32)
    w1 = np.ascontiguousarray(w1_.reshape(C, 128, C, 128).transpose(2, 1, 0, 3).reshape(C, 128, D))
    w2t = np.ascontiguousarray(np.asarray(L0_attn_w2, dtype=np.float32).T)
    mask = np.ascontiguousarray(np.asarray(L0_attn_sparse_mask, dtype=np.float32))
    wbt_ = np.asarray(L0_ffn_W_base, dtype=np.float32).T
    wbt = np.ascontiguousarray(wbt_.reshape(C, 128, C, 128).transpose(2, 1, 0, 3).reshape(C, 128, D))
    b2c = np.ascontiguousarray(np.asarray(L0_attn_b2, dtype=np.float32).reshape(C, 128).T)
    bbc = np.ascontiguousarray(np.asarray(L0_ffn_b_base, dtype=np.float32).reshape(C, 128).T)
    A = np.asarray(L0_ffn_A, dtype=np.float32)
    B = np.asarray(L0_ffn_B, dtype=np.float32)
    Wr = np.asarray(L0_ffn_W_router, dtype=np.float32)
    at_r = np.ascontiguousarray(
        A.reshape(128, D).T.reshape(C, 128, 128).transpose(1, 0, 2).reshape(128, D))
    bt_r = np.ascontiguousarray(B.transpose(0, 2, 1).reshape(128, D))
    wrt_r = np.ascontiguousarray(
        Wr.T.reshape(C, 128, 8).transpose(1, 0, 2).reshape(128, 256))
    e16 = np.zeros((8, 128), dtype=np.float32)
    for e in range(8):
        e16[e, e * 16:(e + 1) * 16] = 1.0
    ident = np.eye(128, dtype=np.float32)

    nc = build_nc()
    in_maps = []
    for r in range(NCORE):
        xs = np.ascontiguousarray(x[0, r * TPC:(r + 1) * TPC, :].T)
        w2t_r = np.ascontiguousarray(
            w2t[:, r * 512:(r + 1) * 512].reshape(C, 128, 512).transpose(1, 0, 2).reshape(128, C * 512))
        in_maps.append({
            "x_fm": xs, "w1_r": w1, "w2t_r": w2t_r,
            "mask_slc": np.ascontiguousarray(mask[:, r * 512:(r + 1) * 512]),
            "wbt_r": wbt, "at_r": at_r, "bt_r": bt_r, "wrt_r": wrt_r,
            "b2c": b2c, "bbc": bbc, "e16": e16, "ident": ident,
        })
    res = run_bass_kernel_spmd(nc, in_maps, core_ids=list(range(NCORE)))
    out = np.empty((1, NCORE * TPC, D), dtype=np.float32)
    for r in range(NCORE):
        out[0, r * TPC:(r + 1) * TPC, :] = res.results[r]["out_fm"].T
    return out


# version 20
# speedup vs baseline: 1.0364x; 1.0092x over previous
import numpy as np
import concourse.bass as bass
import concourse.mybir as mybir
from concourse.bass_utils import run_bass_kernel_spmd

F32 = mybir.dt.float32
BF16 = mybir.dt.bfloat16
AF = mybir.ActivationFunctionType
OP = mybir.AluOpType
AX = mybir.AxisListType

D = 4096
C = 32
NCORE = 8
TPC = 2048
NT = 512
NP = TPC // NT
SM = 13
INV_D = 1.0 / D
EPS = 1e-5


def a1_cnt(p, c):
    return (c + 1) if p == 0 else 96 + 128 * (p - 1) + c + 1


def a2_cnt(p, c):
    return (32 if p == 0 else 128 * p + 32) + c + 1


def a3_cnt(p, c):
    return a2_cnt(p, c) + 32


def a4_cnt(p, c):
    return (128 * (p + 1) if p < 3 else 480) + c + 1


def jsq(p, k, c):
    return 96 * p + 32 * (k - 2) + c


def build_nc():
    nc = bass.Bass("TRN2", target_bir_lowering=False, debug=False)
    x_d = nc.dram_tensor("x_fm", [D, TPC], F32, kind="ExternalInput").ap()
    w1_d = nc.dram_tensor("w1_r", [C, 128, D], F32, kind="ExternalInput").ap()
    w2t_d = nc.dram_tensor("w2t_r", [128, C * NT], F32, kind="ExternalInput").ap()
    msk_d = nc.dram_tensor("mask_slc", [D, NT], F32, kind="ExternalInput").ap()
    wbt_d = nc.dram_tensor("wbt_r", [C, 128, D], F32, kind="ExternalInput").ap()
    at_d = nc.dram_tensor("at_r", [128, D], F32, kind="ExternalInput").ap()
    bt_d = nc.dram_tensor("bt_r", [128, D], F32, kind="ExternalInput").ap()
    wrt_d = nc.dram_tensor("wrt_r", [128, 256], F32, kind="ExternalInput").ap()
    b2c_d = nc.dram_tensor("b2c", [128, C], F32, kind="ExternalInput").ap()
    bbc_d = nc.dram_tensor("bbc", [128, C], F32, kind="ExternalInput").ap()
    e16_d = nc.dram_tensor("e16", [8, 128], F32, kind="ExternalInput").ap()
    id_d = nc.dram_tensor("ident", [128, 128], F32, kind="ExternalInput").ap()
    out_d = nc.dram_tensor("out_fm", [D, TPC], F32, kind="ExternalOutput").ap()
    SPL = 20
    asl_a = nc.dram_tensor("aslice_a", [4 * 128, 128 * SPL], BF16, kind="Internal")
    asl_b = nc.dram_tensor("aslice_b", [4 * 128, 128 * (C - SPL)], BF16, kind="Internal")
    g1_d = nc.dram_tensor("gath1", [NCORE * 4 * 128, 128 * SPL], BF16, kind="Internal", addr_space="Shared")
    g2_d = nc.dram_tensor("gath2", [NCORE * 4 * 128, 128 * (C - SPL)], BF16, kind="Internal", addr_space="Shared")
    wbb_d = nc.dram_tensor("wbb", [C, 128, D], BF16, kind="Internal")

    from contextlib import ExitStack
    with ExitStack() as ctx:
        E = ctx.enter_context
        XP = [E(nc.sbuf_tensor(f"XP{i}", [128, C * NT], BF16)) for i in range(2)]
        U = E(nc.sbuf_tensor("U", [128, C * NT], BF16))
        AB = E(nc.sbuf_tensor("AB", [128, C * NT], BF16))
        SLA = E(nc.sbuf_tensor("SLA", [128, 2 * D], BF16))
        SLW = E(nc.sbuf_tensor("SLW", [128, 2 * D], BF16))
        SQ = E(nc.sbuf_tensor("SQ", [128, 2 * NT], BF16))
        SQ1 = E(nc.sbuf_tensor("SQ1", [128, 2 * NT], BF16))
        T1 = E(nc.sbuf_tensor("T1", [128, NT], BF16))
        OUTS = E(nc.sbuf_tensor("OUTS", [128, 4 * NT], F32))
        FEV = E(nc.sbuf_tensor("FEV", [128, 2 * NT], BF16))
        HW = E(nc.sbuf_tensor("HW", [128, NT], BF16))
        ARN = E(nc.sbuf_tensor("ARN", [128, 2048], F32))
        ATB = E(nc.sbuf_tensor("ATB", [128, D], BF16))
        BTB = E(nc.sbuf_tensor("BTB", [128, D], BF16))
        WRTB = E(nc.sbuf_tensor("WRTB", [128, 256], BF16))
        B2C = E(nc.sbuf_tensor("B2C", [128, C], F32))
        BBC = E(nc.sbuf_tensor("BBC", [128, C], F32))
        E16B = E(nc.sbuf_tensor("E16B", [8, 128], F32))
        IDB = E(nc.sbuf_tensor("IDB", [128, 128], F32))
        ONEC = E(nc.sbuf_tensor("ONEC", [128, 1], BF16))
        ONER = E(nc.sbuf_tensor("ONER", [1, 128], F32))
        MB = E(nc.sbuf_tensor("MB", [128, NT], BF16))
        RB = E(nc.sbuf_tensor("RB", [128, NT], BF16))
        PB = [E(nc.psum_tensor(f"PB{i}", [128, 512], F32)) for i in range(4)]
        PM = E(nc.psum_tensor("PM", [128, 512], F32))
        PR = E(nc.psum_tensor("PR", [128, 512], F32))
        PH = E(nc.psum_tensor("PH", [128, 512], F32))
        PX = E(nc.psum_tensor("PX", [128, 512], F32))
        sem_names = [
            "s_gsm", "s_ssm", "s_w2t", "s_fw1_0", "s_fw1_1", "s_fmm", "s_fev",
            "s_fst_0", "s_fst_1", "s_msk_0", "s_msk_1", "s_cc",
            "s_xin_0", "s_xin_1", "s_xin_2", "s_xin_3",
            "s_xin_4", "s_xin_5", "s_xin_6", "s_xin_7",
            "s_out_0", "s_out_1", "s_out_2", "s_out_3",
            "s_sla_0", "s_sla_1", "s_slw_0", "s_slw_1", "s_wbb", "s_ones",
            "s_cast", "s_sqr", "s_psq", "s_sqr1", "s_psq1",
            "s_fin", "s_rstd", "s_bc", "s_app", "s_mbc",
            "s_ymm", "s_evac", "s_rlg", "s_hmm", "s_hwv", "s_cwb",
            "s_tp1", "s_tp2", "s_rexp", "s_sm",
        ]
        S = {n: E(nc.semaphore(n)) for n in sem_names}
        block = E(nc.Block())

        mean = ARN[0:1, 0:512]
        var = ARN[0:1, 512:1024]
        msq = ARN[0:1, 1024:1536]
        rstd = ARN[0:1, 1536:2048]
        lgsb = ARN[0:8, 0:512]
        cfm = ARN[0:8, 512:1024]
        CW = ARN[:, 1024:1536]
        lgtm = ARN[:, 1536:1568]
        M1 = ARN[:, 1568:1572]
        M2 = ARN[:, 1572:1576]
        e2v = ARN[:, 1576:1580]
        rec = ARN[:, 1580:1584]
        lm = ARN[:, 1584:1616]
        mskS = ARN[:, 1616:1648]
        sel = ARN[:, 1648:1680]
        elm = ARN[:, 1680:1712]
        comb = ARN[:, 1712:1744]
        MB2 = FEV[:, 0:NT]
        RB2 = FEV[:, NT:2 * NT]
        MK = [OUTS[:, 0:NT], OUTS[:, NT:2 * NT]]

        def uch(c):
            return U[:, NT * c:NT * (c + 1)]

        def xpc(p, c):
            return XP[p % 2][:, NT * c:NT * (c + 1)]

        def abch(c):
            return AB[:, NT * c:NT * (c + 1)]

        def sla_t(b, s):
            return SLA[:, D * b + 128 * s: D * b + 128 * s + 128]

        def slw_t(b, s):
            return SLW[:, D * b + 128 * s: D * b + 128 * s + 128]

        def sqs(j):
            return SQ[:, NT * (j % 2): NT * (j % 2) + NT]

        def sq1s(c):
            return SQ1[:, NT * (c % 2): NT * (c % 2) + NT]

        def xin_sem(p, g):
            return S[f"s_xin_{(p % 2) * 4 + g}"]

        xr = x_d.rearrange("(c p) n -> p c n", p=128)
        outr = out_d.rearrange("(c p) n -> p c n", p=128)

        @block.gpsimd
        def _(g):
            g.dma_start(out=ATB[:, :], in_=at_d[:, :]).then_inc(S["s_gsm"], 16)
            g.dma_start(out=BTB[:, :], in_=bt_d[:, :]).then_inc(S["s_gsm"], 16)
            g.dma_start(out=WRTB[:, :], in_=wrt_d[:, :]).then_inc(S["s_gsm"], 16)
            g.dma_start(out=AB[:, :], in_=w2t_d[:, :]).then_inc(S["s_w2t"], 16)
            for sc in range(2):
                g.dma_start(out=SLA[:, D * sc:D * (sc + 1)],
                            in_=w1_d[sc, :, :]).then_inc(S[f"s_fw1_{sc % 2}"], 16)
            def xp_load(pp, gg):
                g.dma_start(
                    out=XP[pp][:, 4096 * gg:4096 * (gg + 1)].rearrange("p (c n) -> p c n", n=NT),
                    in_=xr[:, 8 * gg:8 * (gg + 1), NT * pp:NT * (pp + 1)],
                ).then_inc(xin_sem(pp, gg), 16)
            xp_load(0, 0)
            for sc in range(2, C):
                g.wait_ge(S["s_fmm"], sc - 1)
                g.dma_start(out=SLA[:, D * (sc % 2):D * (sc % 2 + 1)],
                            in_=w1_d[sc, :, :]).then_inc(S[f"s_fw1_{sc % 2}"], 16)
                if sc in (3, 5, 7):
                    xp_load(0, (sc - 1) // 2)
                if sc == 27:
                    g.wait_ge(S["s_fst_0"], 16 * 4 * 10)
                    g.wait_ge(S["s_fst_1"], 16 * 4 * 10)
                    g.collective_compute(
                        "AllGather", OP.bypass, replica_groups=[list(range(NCORE))],
                        ins=[asl_a.ap()], outs=[g1_d.ap()],
                    ).then_inc(S["s_cc"])
            g.wait_ge(S["s_fst_0"], 16 * 4 * (C // 2))
            g.wait_ge(S["s_fst_1"], 16 * 4 * (C // 2))
            g.collective_compute(
                "AllGather", OP.bypass, replica_groups=[list(range(NCORE))],
                ins=[asl_b.ap()], outs=[g2_d.ap()],
            ).then_inc(S["s_cc"])
            for gg in range(4):
                xp_load(1, gg)
            def out_store(p, c):
                g.wait_ge(S["s_app"], a4_cnt(p, c))
                g.dma_start(out=outr[:, c, NT * p:NT * (p + 1)],
                            in_=OUTS[:, NT * (c % 4):NT * (c % 4 + 1)]
                            ).then_inc(S[f"s_out_{c % 4}"], 16)
            for p in range(NP):
                for o in range(C):
                    k = C * p + o
                    if o == 2 and p >= 1:
                        for c in range(C):
                            out_store(p - 1, c)
                            if c % 8 == 7 and p + 1 < NP:
                                gg = c // 8
                                g.dma_start(
                                    out=XP[(p - 1) % 2][:, 4096 * gg:4096 * (gg + 1)].rearrange("p (c n) -> p c n", n=NT),
                                    in_=xr[:, 8 * gg:8 * (gg + 1), NT * (p + 1):NT * (p + 2)],
                                ).then_inc(xin_sem(p + 1, gg), 16)
                    if k >= 2:
                        p2, o2 = divmod(k - 2, C)
                        g.wait_ge(S["s_ymm"], 64 * p2 + 32 + o2 + 1)
                    if p == 0:
                        if k >= 2:
                            g.wait_ge(S["s_wbb"], 16 * (o - 1))
                        g.dma_start(out=SLW[:, D * (k % 2):D * (k % 2 + 1)],
                                    in_=wbt_d[o, :, :]).then_inc(S[f"s_slw_{k % 2}"], 16)
                        g.wait_ge(S[f"s_slw_{k % 2}"], 16 * (k // 2 + 1))
                        g.dma_start(out=wbb_d.ap()[o, :, :],
                                    in_=SLW[:, D * (k % 2):D * (k % 2 + 1)]
                                    ).then_inc(S["s_wbb"], 16)
                    else:
                        g.dma_start(out=SLW[:, D * (k % 2):D * (k % 2 + 1)],
                                    in_=wbb_d.ap()[o, :, :]).then_inc(S[f"s_slw_{k % 2}"], 16)
            for c in range(C):
                out_store(NP - 1, c)
            g.wait_ge(S["s_wbb"], 16 * C)
            for sl in range(4):
                g.wait_ge(S[f"s_out_{sl}"], 16 * 8 * NP)

        @block.sync
        def _(sy):
            sy.dma_start(out=B2C[:, :], in_=b2c_d[:, :]).then_inc(S["s_ssm"], 16)
            sy.dma_start(out=BBC[:, :], in_=bbc_d[:, :]).then_inc(S["s_ssm"], 16)
            sy.dma_start(out=E16B[:, :], in_=e16_d[:, :]).then_inc(S["s_ssm"], 16)
            sy.dma_start(out=IDB[:, :], in_=id_d[:, :]).then_inc(S["s_ssm"], 16)
            for sc in range(2):
                sy.dma_start(out=MK[sc], in_=msk_d[128 * sc:128 * (sc + 1), :]
                             ).then_inc(S[f"s_msk_{sc % 2}"], 16)
            for sc in range(C):
                sy.wait_ge(S["s_fev"], sc + 1)
                asl_h = asl_a if sc < 20 else asl_b
                scl = sc if sc < 20 else sc - 20
                for tl in range(4):
                    sy.dma_start(
                        out=asl_h.ap()[128 * tl:128 * (tl + 1), 128 * scl:128 * (scl + 1)],
                        in_=FEV[:, NT * (sc % 2) + 128 * tl:NT * (sc % 2) + 128 * (tl + 1)],
                    ).then_inc(S[f"s_fst_{sc % 2}"], 16)
                if sc + 2 < C:
                    sy.dma_start(out=MK[(sc + 2) % 2],
                                 in_=msk_d[128 * (sc + 2):128 * (sc + 3), :]
                                 ).then_inc(S[f"s_msk_{sc % 2}"], 16)
            sy.wait_ge(S["s_cc"], 1)
            g1r = g1_d.ap()
            g2r = g2_d.ap()
            for p in range(NP):
                for t in range(C):
                    k = C * p + t
                    if k >= 2:
                        p2, t2 = divmod(k - 2, C)
                        sy.wait_ge(S["s_ymm"], 64 * p2 + t2 + 1)
                    rblk, tl = divmod(t, 4)
                    row0 = rblk * 512 + tl * 128
                    sy.dma_start(out=SLA[:, D * (k % 2):D * (k % 2) + 128 * 20],
                                 in_=g1r[row0:row0 + 128, :]).then_inc(S[f"s_sla_{k % 2}"], 16)
                    if k == 0:
                        sy.wait_ge(S["s_cc"], 2)
                    sy.dma_start(out=SLA[:, D * (k % 2) + 128 * 20:D * (k % 2 + 1)],
                                 in_=g2r[row0:row0 + 128, :]).then_inc(S[f"s_sla_{k % 2}"], 16)
            sy.wait_ge(S["s_sla_0"], 2 * 16 * NP * C // 2)
            sy.wait_ge(S["s_sla_1"], 2 * 16 * NP * C // 2)

        @block.tensor
        def _(t):
            def ln1_stats(p, c):
                t.wait_ge(xin_sem(p, c // 8), 16 * (p // 2 + 1))
                if c == 0 and p > 0:
                    t.wait_ge(S["s_sm"], SM * (p - 1) + 13)
                t.matmul(PX[0:1, 0:NT], ONEC[:, 0:1], xpc(p, c),
                         start=(c == 0), stop=(c == C - 1))
                t.wait_ge(S["s_sqr1"], 32 * p + c + 1)
                t.matmul(PX[32:33, 0:NT], ONEC[:, 0:1], sq1s(c),
                         start=(c == 0), stop=(c == C - 1)).then_inc(S["s_psq1"], 1)

            def bcast(fin_v, mbc_v):
                t.wait_ge(S["s_fin"], fin_v)
                t.wait_ge(S["s_rstd"], fin_v)
                if mbc_v > 0:
                    t.wait_ge(S["s_mbc"], mbc_v)
                t.matmul(PM[:, 0:NT], ONER[0:1, :], mean, start=True, stop=True)
                t.matmul(PR[:, 0:NT], ONER[0:1, :], rstd, start=True, stop=True
                         ).then_inc(S["s_bc"], 1)

            t.wait_ge(S["s_w2t"], 16)
            t.wait_ge(S["s_ones"], 1)
            for sc in range(C):
                t.wait_ge(S[f"s_fw1_{sc % 2}"], 16 * (sc // 2 + 1))
                if sc >= 4:
                    t.wait_ge(S["s_fev"], sc - 3)
                last = None
                for dc in range(C):
                    last = t.matmul(PB[sc % 4][:, 0:NT], sla_t(sc % 2, dc), abch(dc),
                                    start=(dc == 0), stop=(dc == C - 1))
                last.then_inc(S["s_fmm"], 1)
                if sc >= 2:
                    ln1_stats(0, sc - 2)
            for c in range(C - 2, C):
                ln1_stats(0, c)
            t.wait_ge(S["s_ssm"], 64)
            t.wait_ge(S["s_gsm"], 48)
            bcast(1, 0)
            for p in range(NP):
                L = 4 * p

                def ln2_stats(c):
                    t.wait_ge(S["s_evac"], 64 * p + c + 1)
                    if c == 0 and L > 0:
                        t.wait_ge(S["s_fin"], L)
                    t.matmul(PH[0:1, 0:NT], ONEC[:, 0:1], abch(c),
                             start=(c == 0), stop=(c == C - 1))
                    t.wait_ge(S["s_sqr"], jsq(p, 2, c) + 1)
                    t.matmul(PH[32:33, 0:NT], ONEC[:, 0:1], sqs(jsq(p, 2, c)),
                             start=(c == 0), stop=(c == C - 1)).then_inc(S["s_psq"], 1)

                for tt in range(C):
                    k = C * p + tt
                    t.wait_ge(S[f"s_sla_{k % 2}"], 2 * 16 * (k // 2 + 1))
                    if 64 * p + tt - 3 > 0:
                        t.wait_ge(S["s_evac"], 64 * p + tt - 3)
                    last = None
                    for s in range(C):
                        if tt == 0:
                            t.wait_ge(S["s_app"], a1_cnt(p, s))
                        last = t.matmul(PB[tt % 4][:, 0:NT], sla_t(tt % 2, s), uch(s),
                                        start=(s == 0), stop=(s == C - 1))
                    last.then_inc(S["s_ymm"], 1)
                    if tt >= 1:
                        ln2_stats(tt - 1)
                ln2_stats(31)
                bcast(L + 2, L + 1)
                for c in range(C):
                    t.wait_ge(S["s_app"], a2_cnt(p, c))
                    if c == 0:
                        t.wait_ge(S["s_fin"], L + 2)
                    t.matmul(PH[0:1, 0:NT], ONEC[:, 0:1], xpc(p, c),
                             start=(c == 0), stop=(c == C - 1))
                    t.wait_ge(S["s_sqr"], jsq(p, 3, c) + 1)
                    t.matmul(PH[32:33, 0:NT], ONEC[:, 0:1], sqs(jsq(p, 3, c)),
                             start=(c == 0), stop=(c == C - 1)).then_inc(S["s_psq"], 1)
                bcast(L + 3, L + 2)
                if p > 0:
                    t.wait_ge(S["s_sm"], SM * p)
                    t.wait_ge(S["s_fin"], L + 1)
                last = None
                for c in range(C):
                    t.wait_ge(S["s_app"], a3_cnt(p, c))
                    last = t.matmul(PX[0:8, 0:NT], WRTB[:, 8 * c:8 * (c + 1)], uch(c),
                                    start=(c == 0), stop=(c == C - 1))
                last.then_inc(S["s_rlg"], 1)
                t.wait_ge(S["s_fin"], L + 3)
                if p > 0:
                    t.wait_ge(S["s_hwv"], p)
                last = None
                for c in range(C):
                    last = t.matmul(PH[:, 0:NT], ATB[:, 128 * c:128 * (c + 1)], uch(c),
                                    start=(c == 0), stop=(c == C - 1))
                last.then_inc(S["s_hmm"], 1)
                t.wait_ge(S["s_sm"], SM * p + 1)
                last = None
                for gg in range(4):
                    last = t.transpose(PX[:, 8 * gg:8 * (gg + 1)],
                                       lgsb[:, 128 * gg:128 * (gg + 1)], IDB[0:8, 0:8])
                last.then_inc(S["s_tp1"], 1)

                def f_base(o):
                    k = C * p + o
                    t.wait_ge(S[f"s_slw_{k % 2}"], 16 * (k // 2 + 1))
                    t.wait_ge(S["s_evac"], 64 * p + 29 + o)
                    for s in range(C):
                        t.matmul(PB[o % 4][:, 0:NT], slw_t(o % 2, s), uch(s),
                                 start=(s == 0), stop=False)

                def f_expert(o):
                    t.matmul(PB[o % 4][:, 0:NT], BTB[:, 128 * o:128 * (o + 1)], HW[:, :],
                             start=False, stop=True).then_inc(S["s_ymm"], 1)

                def ln4_stats(c):
                    t.wait_ge(S["s_evac"], 64 * p + 32 + c + 1)
                    if c == 0:
                        t.wait_ge(S["s_fin"], L + 3)
                        t.wait_ge(S["s_hwv"], p + 1)
                    t.matmul(PH[0:1, 0:NT], ONEC[:, 0:1], abch(c),
                             start=(c == 0), stop=(c == C - 1))
                    t.wait_ge(S["s_sqr"], jsq(p, 4, c) + 1)
                    t.matmul(PH[32:33, 0:NT], ONEC[:, 0:1], sqs(jsq(p, 4, c)),
                             start=(c == 0), stop=(c == C - 1)).then_inc(S["s_psq"], 1)
                f_base(0)
                f_base(1)
                t.wait_ge(S["s_sm"], SM * p + 11)
                last = None
                for gg in range(4):
                    last = t.transpose(PX[0:8, 128 * gg:128 * (gg + 1)],
                                       comb[:, 8 * gg:8 * (gg + 1)], IDB[:, :])
                last.then_inc(S["s_tp2"], 1)
                t.wait_ge(S["s_sm"], SM * p + 12)
                t.matmul(PX[:, 0:NT], E16B[:, :], cfm, start=True, stop=True
                         ).then_inc(S["s_cwb"], 1)
                t.wait_ge(S["s_hwv"], p + 1)
                f_expert(0)
                f_expert(1)
                ln4_stats(0)
                for o in range(2, C):
                    f_base(o)
                    f_expert(o)
                    ln4_stats(o - 1)
                    if p + 1 < NP:
                        ln1_stats(p + 1, o - 2)
                ln4_stats(31)
                if p + 1 < NP:
                    ln1_stats(p + 1, 30)
                    ln1_stats(p + 1, 31)
                bcast(L + 4, L + 3)
                if p + 1 < NP:
                    bcast(L + 5, L + 4)

        @block.scalar
        def _(s):
            for c in range(C):
                s.wait_ge(xin_sem(0, c // 8), 16)
                if c >= 2:
                    s.wait_ge(S["s_psq1"], c - 1)
                s.activation(sq1s(c), xpc(0, c), AF.Square).then_inc(S["s_sqr1"], 1)
            for p in range(NP):
                L = 4 * p
                if p == 0:
                    s.wait_ge(S["s_fin"], 1)
                    s.activation(var, var, AF.Ln)
                    s.activation(rstd, var, AF.Exp, scale=-0.5).then_inc(S["s_rstd"], 1)
                if p == 0:
                    s.wait_ge(S["s_fmm"], C)
                    s.wait_ge(S["s_ssm"], 64)
                for tt in range(C):
                    s.wait_ge(S["s_ymm"], 64 * p + tt + 1)
                    if p > 0:
                        s.wait_ge(S["s_app"], a4_cnt(p - 1, tt))
                    s.activation(abch(tt), PB[tt % 4][:, 0:NT], AF.Gelu,
                                 bias=B2C[:, tt:tt + 1]).then_inc(S["s_evac"], 1)
                    j = jsq(p, 2, tt)
                    if j - 2 >= 0:
                        s.wait_ge(S["s_psq"], j - 1)
                    s.activation(sqs(j), abch(tt), AF.Square).then_inc(S["s_sqr"], 1)
                s.wait_ge(S["s_fin"], L + 2)
                s.wait_ge(S["s_bc"], L + 1)
                s.activation(var, var, AF.Ln)
                s.activation(rstd, var, AF.Exp, scale=-0.5).then_inc(S["s_rstd"], 1)
                for c in range(C):
                    s.wait_ge(S["s_app"], a2_cnt(p, c))
                    j = jsq(p, 3, c)
                    s.wait_ge(S["s_psq"], j - 1)
                    s.activation(sqs(j), xpc(p, c), AF.Square).then_inc(S["s_sqr"], 1)
                s.wait_ge(S["s_fin"], L + 3)
                s.wait_ge(S["s_bc"], L + 2)
                s.activation(var, var, AF.Ln)
                s.activation(rstd, var, AF.Exp, scale=-0.5).then_inc(S["s_rstd"], 1)
                s.wait_ge(S["s_sm"], SM * p + 8)
                s.activation(elm, lm, AF.Exp)
                s.activation(e2v, M2, AF.Exp).then_inc(S["s_rexp"], 1)
                for o in range(C):
                    s.wait_ge(S["s_ymm"], 64 * p + 32 + o + 1)
                    s.activation(abch(o), PB[o % 4][:, 0:NT], AF.Identity,
                                 bias=BBC[:, o:o + 1]).then_inc(S["s_evac"], 1)
                    j = jsq(p, 4, o)
                    s.wait_ge(S["s_psq"], j - 1)
                    s.activation(sqs(j), abch(o), AF.Square).then_inc(S["s_sqr"], 1)
                    if p + 1 < NP:
                        c = o
                        s.wait_ge(xin_sem(p + 1, c // 8), 16 * ((p + 1) // 2 + 1))
                        s.wait_ge(S["s_psq1"], 32 * (p + 1) + c - 1)
                        s.activation(sq1s(c), xpc(p + 1, c), AF.Square).then_inc(S["s_sqr1"], 1)
                s.wait_ge(S["s_fin"], L + 4)
                s.wait_ge(S["s_bc"], L + 3)
                s.activation(var, var, AF.Ln)
                s.activation(rstd, var, AF.Exp, scale=-0.5).then_inc(S["s_rstd"], 1)
                if p + 1 < NP:
                    s.wait_ge(S["s_fin"], L + 5)
                    s.wait_ge(S["s_bc"], L + 4)
                    s.activation(var, var, AF.Ln)
                    s.activation(rstd, var, AF.Exp, scale=-0.5).then_inc(S["s_rstd"], 1)

        @block.vector
        def _(v):
            v.memset(ONEC[:, :], 1.0)
            v.memset(ONER[:, :], 1.0).then_inc(S["s_ones"], 1)
            for sc in range(C):
                v.wait_ge(S["s_fmm"], sc + 1)
                v.wait_ge(S[f"s_msk_{sc % 2}"], 16 * (sc // 2 + 1))
                if sc >= 2:
                    v.wait_ge(S[f"s_fst_{sc % 2}"], 16 * 4 * (sc // 2))
                v.tensor_tensor(FEV[:, NT * (sc % 2):NT * (sc % 2 + 1)],
                                PB[sc % 4][:, 0:NT], MK[sc % 2],
                                OP.mult).then_inc(S["s_fev"], 1)
            smc = [0]

            def sm_inc(ins):
                smc[0] += 1
                ins.then_inc(S["s_sm"], 1)

            def sm_step(ins):
                sm_inc(ins)
                v.wait_ge(S["s_sm"], smc[0])

            def finalize(bank, gl):
                if gl > 0:
                    v.wait_ge(S["s_bc"], gl)
                v.tensor_scalar(mean, bank[0:1, 0:NT], INV_D, None, OP.mult)
                v.tensor_scalar(var, bank[32:33, 0:NT], INV_D, EPS, OP.mult, OP.add)
                v.tensor_tensor(msq, mean, mean, OP.mult)
                v.tensor_tensor(var, var, msq, OP.subtract).then_inc(S["s_fin"], 1)

            def copies(mb, rb):
                v.tensor_copy(mb, PM[:, 0:NT])
                v.tensor_copy(rb, PR[:, 0:NT]).then_inc(S["s_mbc"], 1)

            def apply1_stream(p):
                for c in range(C):
                    v.wait_ge(xin_sem(p, c // 8), 16 * (p // 2 + 1))
                    v.tensor_tensor(T1[:, :], xpc(p, c), MB2, OP.subtract)
                    v.tensor_tensor(uch(c), T1[:, :], RB2, OP.mult).then_inc(S["s_app"], 1)

            v.wait_ge(S["s_psq1"], C)
            v.wait_ge(S["s_fst_0"], 16 * 4 * (C // 2))
            v.wait_ge(S["s_fst_1"], 16 * 4 * (C // 2))
            finalize(PX, 0)
            v.wait_ge(S["s_bc"], 1)
            copies(MB2, RB2)
            apply1_stream(0)
            for p in range(NP):
                L = 4 * p
                v.wait_ge(S["s_psq"], 96 * p + 32)
                finalize(PH, L + 1)
                v.wait_ge(S["s_bc"], L + 2)
                copies(MB[:, :], RB[:, :])
                for c in range(C):
                    v.tensor_tensor(T1[:, :], abch(c), MB[:, :], OP.subtract)
                    v.tensor_tensor(T1[:, :], T1[:, :], RB[:, :], OP.mult)
                    v.tensor_tensor(xpc(p, c), xpc(p, c), T1[:, :], OP.add).then_inc(S["s_app"], 1)
                v.wait_ge(S["s_psq"], 96 * p + 64)
                finalize(PH, L + 2)
                v.wait_ge(S["s_bc"], L + 3)
                copies(MB[:, :], RB[:, :])
                for c in range(C):
                    v.tensor_tensor(T1[:, :], xpc(p, c), MB[:, :], OP.subtract)
                    v.tensor_tensor(uch(c), T1[:, :], RB[:, :], OP.mult).then_inc(S["s_app"], 1)
                v.wait_ge(S["s_rlg"], p + 1)
                sm_inc(v.tensor_copy(lgsb, PX[0:8, 0:NT]))
                v.wait_ge(S["s_tp1"], p + 1)
                sm_step(v.tensor_copy(lgtm, PX[:, 0:32]))
                sm_step(v.reduce_max(M1, lgtm.rearrange("p (g e) -> p g e", e=8), axis=AX.X))
                last = None
                for gg in range(4):
                    last = v.tensor_scalar(lm[:, 8 * gg:8 * (gg + 1)],
                                           lgtm[:, 8 * gg:8 * (gg + 1)],
                                           M1[:, gg:gg + 1], None, OP.subtract)
                sm_step(last)
                sm_step(v.tensor_scalar(mskS, lm, 0.0, None, OP.is_equal))
                sm_step(v.scalar_tensor_tensor(mskS, mskS, -1e30, lm, OP.mult, OP.add))
                sm_step(v.reduce_max(M2, mskS.rearrange("p (g e) -> p g e", e=8), axis=AX.X))
                last = None
                for gg in range(4):
                    last = v.tensor_scalar(sel[:, 8 * gg:8 * (gg + 1)],
                                           lm[:, 8 * gg:8 * (gg + 1)],
                                           M2[:, gg:gg + 1], None, OP.is_ge)
                sm_inc(last)
                v.wait_ge(S["s_rexp"], p + 1)
                sm_step(v.tensor_scalar(e2v, e2v, 0.5, 0.5, OP.mult, OP.add))
                sm_step(v.reciprocal(rec, e2v))
                last = None
                for gg in range(4):
                    last = v.scalar_tensor_tensor(comb[:, 8 * gg:8 * (gg + 1)],
                                                  elm[:, 8 * gg:8 * (gg + 1)],
                                                  rec[:, gg:gg + 1],
                                                  sel[:, 8 * gg:8 * (gg + 1)],
                                                  OP.mult, OP.mult)
                sm_inc(last)
                v.wait_ge(S["s_tp2"], p + 1)
                sm_inc(v.tensor_copy(cfm, PX[0:8, 0:NT]))
                v.wait_ge(S["s_cwb"], p + 1)
                sm_inc(v.tensor_copy(CW, PX[:, 0:NT]))
                v.wait_ge(S["s_hmm"], p + 1)
                v.tensor_tensor(HW[:, :], PH[:, 0:NT], CW, OP.mult).then_inc(S["s_hwv"], 1)
                v.wait_ge(S["s_psq"], 96 * p + 96)
                finalize(PH, L + 3)
                v.wait_ge(S["s_bc"], L + 4)
                copies(MB[:, :], RB[:, :])
                if p + 1 < NP:
                    v.wait_ge(S["s_psq1"], 32 * (p + 1) + 32)
                    finalize(PX, L + 4)
                    v.wait_ge(S["s_bc"], L + 5)
                    copies(MB2, RB2)
                    apply1_stream(p + 1)
                for c in range(C):
                    v.tensor_tensor(T1[:, :], abch(c), MB[:, :], OP.subtract)
                    v.tensor_tensor(T1[:, :], T1[:, :], RB[:, :], OP.mult)
                    if 8 * p + c // 4 > 0:
                        v.wait_ge(S[f"s_out_{c % 4}"], 16 * (8 * p + c // 4))
                    v.tensor_tensor(OUTS[:, NT * (c % 4):NT * (c % 4 + 1)],
                                    xpc(p, c), T1[:, :], OP.add).then_inc(S["s_app"], 1)
            assert smc[0] == SM * NP, smc[0]
    return nc


def kernel(L0_attn_x, L0_attn_w1, L0_attn_w2, L0_attn_b2, L0_attn_sparse_mask,
           L0_ffn_W_base, L0_ffn_b_base, L0_ffn_W_router, L0_ffn_A, L0_ffn_B):
    x = np.asarray(L0_attn_x, dtype=np.float32)
    w1_ = np.asarray(L0_attn_w1, dtype=np.float32)
    w1 = np.ascontiguousarray(w1_.reshape(C, 128, C, 128).transpose(2, 1, 0, 3).reshape(C, 128, D))
    w2t = np.ascontiguousarray(np.asarray(L0_attn_w2, dtype=np.float32).T)
    mask = np.ascontiguousarray(np.asarray(L0_attn_sparse_mask, dtype=np.float32))
    wbt_ = np.asarray(L0_ffn_W_base, dtype=np.float32).T
    wbt = np.ascontiguousarray(wbt_.reshape(C, 128, C, 128).transpose(2, 1, 0, 3).reshape(C, 128, D))
    b2c = np.ascontiguousarray(np.asarray(L0_attn_b2, dtype=np.float32).reshape(C, 128).T)
    bbc = np.ascontiguousarray(np.asarray(L0_ffn_b_base, dtype=np.float32).reshape(C, 128).T)
    A = np.asarray(L0_ffn_A, dtype=np.float32)
    B = np.asarray(L0_ffn_B, dtype=np.float32)
    Wr = np.asarray(L0_ffn_W_router, dtype=np.float32)
    at_r = np.ascontiguousarray(
        A.reshape(128, D).T.reshape(C, 128, 128).transpose(1, 0, 2).reshape(128, D))
    bt_r = np.ascontiguousarray(B.transpose(0, 2, 1).reshape(128, D))
    wrt_r = np.ascontiguousarray(
        Wr.T.reshape(C, 128, 8).transpose(1, 0, 2).reshape(128, 256))
    e16 = np.zeros((8, 128), dtype=np.float32)
    for e in range(8):
        e16[e, e * 16:(e + 1) * 16] = 1.0
    ident = np.eye(128, dtype=np.float32)

    nc = build_nc()
    in_maps = []
    for r in range(NCORE):
        xs = np.ascontiguousarray(x[0, r * TPC:(r + 1) * TPC, :].T)
        w2t_r = np.ascontiguousarray(
            w2t[:, r * 512:(r + 1) * 512].reshape(C, 128, 512).transpose(1, 0, 2).reshape(128, C * 512))
        in_maps.append({
            "x_fm": xs, "w1_r": w1, "w2t_r": w2t_r,
            "mask_slc": np.ascontiguousarray(mask[:, r * 512:(r + 1) * 512]),
            "wbt_r": wbt, "at_r": at_r, "bt_r": bt_r, "wrt_r": wrt_r,
            "b2c": b2c, "bbc": bbc, "e16": e16, "ident": ident,
        })
    res = run_bass_kernel_spmd(nc, in_maps, core_ids=list(range(NCORE)))
    out = np.empty((1, NCORE * TPC, D), dtype=np.float32)
    for r in range(NCORE):
        out[0, r * TPC:(r + 1) * TPC, :] = res.results[r]["out_fm"].T
    return out


# version 21
# speedup vs baseline: 1.0440x; 1.0074x over previous
import numpy as np
import concourse.bass as bass
import concourse.mybir as mybir
from concourse.bass_utils import run_bass_kernel_spmd

F32 = mybir.dt.float32
BF16 = mybir.dt.bfloat16
AF = mybir.ActivationFunctionType
OP = mybir.AluOpType
AX = mybir.AxisListType

D = 4096
C = 32
NCORE = 8
TPC = 2048
NT = 512
NP = TPC // NT
SM = 13
INV_D = 1.0 / D
EPS = 1e-5


def a1_cnt(p, c):
    return (c + 1) if p == 0 else 96 + 128 * (p - 1) + c + 1


def a2_cnt(p, c):
    return (32 if p == 0 else 128 * p + 32) + c + 1


def a3_cnt(p, c):
    return a2_cnt(p, c) + 32


def a4_cnt(p, c):
    return (128 * (p + 1) if p < 3 else 480) + c + 1


def jsq(p, k, c):
    return 96 * p + 32 * (k - 2) + c


def build_nc():
    nc = bass.Bass("TRN2", target_bir_lowering=False, debug=False)
    x_d = nc.dram_tensor("x_fm", [D, TPC], F32, kind="ExternalInput").ap()
    w1_d = nc.dram_tensor("w1_r", [C, 128, D], F32, kind="ExternalInput").ap()
    w2t_d = nc.dram_tensor("w2t_r", [128, C * NT], F32, kind="ExternalInput").ap()
    msk_d = nc.dram_tensor("mask_slc", [D, NT], F32, kind="ExternalInput").ap()
    wbt_d = nc.dram_tensor("wbt_r", [C, 128, D], F32, kind="ExternalInput").ap()
    at_d = nc.dram_tensor("at_r", [128, D], F32, kind="ExternalInput").ap()
    bt_d = nc.dram_tensor("bt_r", [128, D], F32, kind="ExternalInput").ap()
    wrt_d = nc.dram_tensor("wrt_r", [128, 256], F32, kind="ExternalInput").ap()
    b2c_d = nc.dram_tensor("b2c", [128, C], F32, kind="ExternalInput").ap()
    bbc_d = nc.dram_tensor("bbc", [128, C], F32, kind="ExternalInput").ap()
    e16_d = nc.dram_tensor("e16", [8, 128], F32, kind="ExternalInput").ap()
    id_d = nc.dram_tensor("ident", [128, 128], F32, kind="ExternalInput").ap()
    out_d = nc.dram_tensor("out_fm", [D, TPC], F32, kind="ExternalOutput").ap()
    SPL = 20
    asl_a = nc.dram_tensor("aslice_a", [4 * 128, 128 * SPL], BF16, kind="Internal")
    asl_b = nc.dram_tensor("aslice_b", [4 * 128, 128 * (C - SPL)], BF16, kind="Internal")
    g1_d = nc.dram_tensor("gath1", [NCORE * 4 * 128, 128 * SPL], BF16, kind="Internal", addr_space="Shared")
    g2_d = nc.dram_tensor("gath2", [NCORE * 4 * 128, 128 * (C - SPL)], BF16, kind="Internal", addr_space="Shared")
    wbb_d = nc.dram_tensor("wbb", [C, 128, D], BF16, kind="Internal")

    from contextlib import ExitStack
    with ExitStack() as ctx:
        E = ctx.enter_context
        XP = [E(nc.sbuf_tensor(f"XP{i}", [128, C * NT], BF16)) for i in range(2)]
        U = E(nc.sbuf_tensor("U", [128, C * NT], BF16))
        AB = E(nc.sbuf_tensor("AB", [128, C * NT], BF16))
        SLA = E(nc.sbuf_tensor("SLA", [128, 2 * D], BF16))
        SLW = E(nc.sbuf_tensor("SLW", [128, 2 * D], BF16))
        SQ = E(nc.sbuf_tensor("SQ", [128, 2 * NT], BF16))
        SQ1 = E(nc.sbuf_tensor("SQ1", [128, 2 * NT], BF16))
        T1 = E(nc.sbuf_tensor("T1", [128, NT], BF16))
        OUTS = E(nc.sbuf_tensor("OUTS", [128, 4 * NT], F32))
        FEV = E(nc.sbuf_tensor("FEV", [128, 2 * NT], BF16))
        HW = E(nc.sbuf_tensor("HW", [128, NT], BF16))
        ARN = E(nc.sbuf_tensor("ARN", [128, 2048], F32))
        ATB = E(nc.sbuf_tensor("ATB", [128, D], BF16))
        BTB = E(nc.sbuf_tensor("BTB", [128, D], BF16))
        WRTB = E(nc.sbuf_tensor("WRTB", [128, 256], BF16))
        B2C = E(nc.sbuf_tensor("B2C", [128, C], F32))
        BBC = E(nc.sbuf_tensor("BBC", [128, C], F32))
        E16B = E(nc.sbuf_tensor("E16B", [8, 128], F32))
        IDB = E(nc.sbuf_tensor("IDB", [128, 128], F32))
        ONEC = E(nc.sbuf_tensor("ONEC", [128, 1], BF16))
        ONER = E(nc.sbuf_tensor("ONER", [1, 128], F32))
        MB = E(nc.sbuf_tensor("MB", [128, NT], BF16))
        RB = E(nc.sbuf_tensor("RB", [128, NT], BF16))
        PB = [E(nc.psum_tensor(f"PB{i}", [128, 512], F32)) for i in range(4)]
        PM = E(nc.psum_tensor("PM", [128, 512], F32))
        PR = E(nc.psum_tensor("PR", [128, 512], F32))
        PH = E(nc.psum_tensor("PH", [128, 512], F32))
        PX = E(nc.psum_tensor("PX", [128, 512], F32))
        sem_names = [
            "s_gsm", "s_ssm", "s_w2t", "s_fw1_0", "s_fw1_1", "s_fmm", "s_fev",
            "s_fst_0", "s_fst_1", "s_msk_0", "s_msk_1", "s_cc",
            "s_xin_0", "s_xin_1", "s_xin_2", "s_xin_3",
            "s_xin_4", "s_xin_5", "s_xin_6", "s_xin_7",
            "s_out_0", "s_out_1", "s_out_2", "s_out_3",
            "s_sla_0", "s_sla_1", "s_slw_0", "s_slw_1", "s_wbb", "s_ones",
            "s_cast", "s_sqr", "s_psq", "s_sqr1", "s_psq1",
            "s_fin", "s_rstd", "s_bc", "s_app", "s_mbc",
            "s_ymm", "s_evac", "s_rlg", "s_hmm", "s_hwv", "s_cwb",
            "s_tp1", "s_tp2", "s_rexp", "s_sm",
        ]
        S = {n: E(nc.semaphore(n)) for n in sem_names}
        block = E(nc.Block())

        mean = ARN[0:1, 0:512]
        var = ARN[0:1, 512:1024]
        msq = ARN[0:1, 1024:1536]
        rstd = ARN[0:1, 1536:2048]
        lgsb = ARN[0:8, 0:512]
        cfm = ARN[0:8, 512:1024]
        CW = ARN[:, 1024:1536]
        lgtm = ARN[:, 1536:1568]
        M1 = ARN[:, 1568:1572]
        M2 = ARN[:, 1572:1576]
        e2v = ARN[:, 1576:1580]
        rec = ARN[:, 1580:1584]
        lm = ARN[:, 1584:1616]
        mskS = ARN[:, 1616:1648]
        sel = ARN[:, 1648:1680]
        elm = ARN[:, 1680:1712]
        comb = ARN[:, 1712:1744]
        MB2 = FEV[:, 0:NT]
        RB2 = FEV[:, NT:2 * NT]
        MK = [OUTS[:, 0:NT], OUTS[:, NT:2 * NT]]

        def uch(c):
            return U[:, NT * c:NT * (c + 1)]

        def xpc(p, c):
            return XP[p % 2][:, NT * c:NT * (c + 1)]

        def abch(c):
            return AB[:, NT * c:NT * (c + 1)]

        def sla_t(b, s):
            return SLA[:, D * b + 128 * s: D * b + 128 * s + 128]

        def slw_t(b, s):
            return SLW[:, D * b + 128 * s: D * b + 128 * s + 128]

        def sqs(j):
            return SQ[:, NT * (j % 2): NT * (j % 2) + NT]

        def sq1s(c):
            return SQ1[:, NT * (c % 2): NT * (c % 2) + NT]

        def xin_sem(p, g):
            return S[f"s_xin_{(p % 2) * 4 + g}"]

        xr = x_d.rearrange("(c p) n -> p c n", p=128)
        outr = out_d.rearrange("(c p) n -> p c n", p=128)

        @block.gpsimd
        def _(g):
            g.dma_start(out=AB[:, :], in_=w2t_d[:, :]).then_inc(S["s_w2t"], 16)
            for sc in range(2):
                g.dma_start(out=SLA[:, D * sc:D * (sc + 1)],
                            in_=w1_d[sc, :, :]).then_inc(S[f"s_fw1_{sc % 2}"], 16)
            def xp_load(pp, gg):
                g.dma_start(
                    out=XP[pp][:, 4096 * gg:4096 * (gg + 1)].rearrange("p (c n) -> p c n", n=NT),
                    in_=xr[:, 8 * gg:8 * (gg + 1), NT * pp:NT * (pp + 1)],
                ).then_inc(xin_sem(pp, gg), 16)
            xp_load(0, 0)
            for sc in range(2, C):
                g.wait_ge(S["s_fmm"], sc - 1)
                g.dma_start(out=SLA[:, D * (sc % 2):D * (sc % 2 + 1)],
                            in_=w1_d[sc, :, :]).then_inc(S[f"s_fw1_{sc % 2}"], 16)
                if sc in (3, 5, 7):
                    xp_load(0, (sc - 1) // 2)
                if sc == 24:
                    g.dma_start(out=ATB[:, :], in_=at_d[:, :]).then_inc(S["s_gsm"], 16)
                    g.dma_start(out=BTB[:, :], in_=bt_d[:, :]).then_inc(S["s_gsm"], 16)
                    g.dma_start(out=WRTB[:, :], in_=wrt_d[:, :]).then_inc(S["s_gsm"], 16)
                if sc == 27:
                    g.wait_ge(S["s_fst_0"], 16 * 4 * 10)
                    g.wait_ge(S["s_fst_1"], 16 * 4 * 10)
                    g.collective_compute(
                        "AllGather", OP.bypass, replica_groups=[list(range(NCORE))],
                        ins=[asl_a.ap()], outs=[g1_d.ap()],
                    ).then_inc(S["s_cc"])
            g.wait_ge(S["s_fst_0"], 16 * 4 * (C // 2))
            g.wait_ge(S["s_fst_1"], 16 * 4 * (C // 2))
            g.collective_compute(
                "AllGather", OP.bypass, replica_groups=[list(range(NCORE))],
                ins=[asl_b.ap()], outs=[g2_d.ap()],
            ).then_inc(S["s_cc"])
            for gg in range(4):
                xp_load(1, gg)
            def out_store(p, c):
                g.wait_ge(S["s_app"], a4_cnt(p, c))
                g.dma_start(out=outr[:, c, NT * p:NT * (p + 1)],
                            in_=OUTS[:, NT * (c % 4):NT * (c % 4 + 1)]
                            ).then_inc(S[f"s_out_{c % 4}"], 16)
            for p in range(NP):
                for o in range(C):
                    k = C * p + o
                    if o == 2 and p >= 1:
                        for c in range(C):
                            out_store(p - 1, c)
                            if c % 8 == 7 and p + 1 < NP:
                                gg = c // 8
                                g.dma_start(
                                    out=XP[(p - 1) % 2][:, 4096 * gg:4096 * (gg + 1)].rearrange("p (c n) -> p c n", n=NT),
                                    in_=xr[:, 8 * gg:8 * (gg + 1), NT * (p + 1):NT * (p + 2)],
                                ).then_inc(xin_sem(p + 1, gg), 16)
                    if k >= 2:
                        p2, o2 = divmod(k - 2, C)
                        g.wait_ge(S["s_ymm"], 64 * p2 + 32 + o2 + 1)
                    if p == 0:
                        if k >= 2:
                            g.wait_ge(S["s_wbb"], 16 * (o - 1))
                        g.dma_start(out=SLW[:, D * (k % 2):D * (k % 2 + 1)],
                                    in_=wbt_d[o, :, :]).then_inc(S[f"s_slw_{k % 2}"], 16)
                        g.wait_ge(S[f"s_slw_{k % 2}"], 16 * (k // 2 + 1))
                        g.dma_start(out=wbb_d.ap()[o, :, :],
                                    in_=SLW[:, D * (k % 2):D * (k % 2 + 1)]
                                    ).then_inc(S["s_wbb"], 16)
                    else:
                        g.dma_start(out=SLW[:, D * (k % 2):D * (k % 2 + 1)],
                                    in_=wbb_d.ap()[o, :, :]).then_inc(S[f"s_slw_{k % 2}"], 16)
            for c in range(C):
                out_store(NP - 1, c)
            g.wait_ge(S["s_wbb"], 16 * C)
            for sl in range(4):
                g.wait_ge(S[f"s_out_{sl}"], 16 * 8 * NP)

        @block.sync
        def _(sy):
            sy.dma_start(out=B2C[:, :], in_=b2c_d[:, :]).then_inc(S["s_ssm"], 16)
            sy.dma_start(out=BBC[:, :], in_=bbc_d[:, :]).then_inc(S["s_ssm"], 16)
            sy.dma_start(out=E16B[:, :], in_=e16_d[:, :]).then_inc(S["s_ssm"], 16)
            sy.dma_start(out=IDB[:, :], in_=id_d[:, :]).then_inc(S["s_ssm"], 16)
            for sc in range(2):
                sy.dma_start(out=MK[sc], in_=msk_d[128 * sc:128 * (sc + 1), :]
                             ).then_inc(S[f"s_msk_{sc % 2}"], 16)
            for sc in range(C):
                sy.wait_ge(S["s_fev"], sc + 1)
                asl_h = asl_a if sc < 20 else asl_b
                scl = sc if sc < 20 else sc - 20
                for tl in range(4):
                    sy.dma_start(
                        out=asl_h.ap()[128 * tl:128 * (tl + 1), 128 * scl:128 * (scl + 1)],
                        in_=FEV[:, NT * (sc % 2) + 128 * tl:NT * (sc % 2) + 128 * (tl + 1)],
                    ).then_inc(S[f"s_fst_{sc % 2}"], 16)
                if sc + 2 < C:
                    sy.dma_start(out=MK[(sc + 2) % 2],
                                 in_=msk_d[128 * (sc + 2):128 * (sc + 3), :]
                                 ).then_inc(S[f"s_msk_{sc % 2}"], 16)
            sy.wait_ge(S["s_cc"], 1)
            g1r = g1_d.ap()
            g2r = g2_d.ap()
            for p in range(NP):
                for t in range(C):
                    k = C * p + t
                    if k >= 2:
                        p2, t2 = divmod(k - 2, C)
                        sy.wait_ge(S["s_ymm"], 64 * p2 + t2 + 1)
                    rblk, tl = divmod(t, 4)
                    row0 = rblk * 512 + tl * 128
                    sy.dma_start(out=SLA[:, D * (k % 2):D * (k % 2) + 128 * 20],
                                 in_=g1r[row0:row0 + 128, :]).then_inc(S[f"s_sla_{k % 2}"], 16)
                    if k == 0:
                        sy.wait_ge(S["s_cc"], 2)
                    sy.dma_start(out=SLA[:, D * (k % 2) + 128 * 20:D * (k % 2 + 1)],
                                 in_=g2r[row0:row0 + 128, :]).then_inc(S[f"s_sla_{k % 2}"], 16)
            sy.wait_ge(S["s_sla_0"], 2 * 16 * NP * C // 2)
            sy.wait_ge(S["s_sla_1"], 2 * 16 * NP * C // 2)

        @block.tensor
        def _(t):
            def ln1_stats(p, c):
                t.wait_ge(xin_sem(p, c // 8), 16 * (p // 2 + 1))
                if c == 0 and p > 0:
                    t.wait_ge(S["s_sm"], SM * (p - 1) + 13)
                t.matmul(PX[0:1, 0:NT], ONEC[:, 0:1], xpc(p, c),
                         start=(c == 0), stop=(c == C - 1))
                t.wait_ge(S["s_sqr1"], 32 * p + c + 1)
                t.matmul(PX[32:33, 0:NT], ONEC[:, 0:1], sq1s(c),
                         start=(c == 0), stop=(c == C - 1)).then_inc(S["s_psq1"], 1)

            def bcast(fin_v, mbc_v):
                t.wait_ge(S["s_fin"], fin_v)
                t.wait_ge(S["s_rstd"], fin_v)
                if mbc_v > 0:
                    t.wait_ge(S["s_mbc"], mbc_v)
                t.matmul(PM[:, 0:NT], ONER[0:1, :], mean, start=True, stop=True)
                t.matmul(PR[:, 0:NT], ONER[0:1, :], rstd, start=True, stop=True
                         ).then_inc(S["s_bc"], 1)

            t.wait_ge(S["s_w2t"], 16)
            t.wait_ge(S["s_ones"], 1)
            for sc in range(C):
                t.wait_ge(S[f"s_fw1_{sc % 2}"], 16 * (sc // 2 + 1))
                if sc >= 4:
                    t.wait_ge(S["s_fev"], sc - 3)
                last = None
                for dc in range(C):
                    last = t.matmul(PB[sc % 4][:, 0:NT], sla_t(sc % 2, dc), abch(dc),
                                    start=(dc == 0), stop=(dc == C - 1))
                last.then_inc(S["s_fmm"], 1)
                if sc >= 2:
                    ln1_stats(0, sc - 2)
            for c in range(C - 2, C):
                ln1_stats(0, c)
            t.wait_ge(S["s_ssm"], 64)
            t.wait_ge(S["s_gsm"], 48)
            bcast(1, 0)
            for p in range(NP):
                L = 4 * p

                def ln2_stats(c):
                    t.wait_ge(S["s_evac"], 64 * p + c + 1)
                    if c == 0 and L > 0:
                        t.wait_ge(S["s_fin"], L)
                    t.matmul(PH[0:1, 0:NT], ONEC[:, 0:1], abch(c),
                             start=(c == 0), stop=(c == C - 1))
                    t.wait_ge(S["s_sqr"], jsq(p, 2, c) + 1)
                    t.matmul(PH[32:33, 0:NT], ONEC[:, 0:1], sqs(jsq(p, 2, c)),
                             start=(c == 0), stop=(c == C - 1)).then_inc(S["s_psq"], 1)

                for tt in range(C):
                    k = C * p + tt
                    t.wait_ge(S[f"s_sla_{k % 2}"], 2 * 16 * (k // 2 + 1))
                    if 64 * p + tt - 3 > 0:
                        t.wait_ge(S["s_evac"], 64 * p + tt - 3)
                    last = None
                    for s in range(C):
                        if tt == 0:
                            t.wait_ge(S["s_app"], a1_cnt(p, s))
                        last = t.matmul(PB[tt % 4][:, 0:NT], sla_t(tt % 2, s), uch(s),
                                        start=(s == 0), stop=(s == C - 1))
                    last.then_inc(S["s_ymm"], 1)
                    if tt >= 1:
                        ln2_stats(tt - 1)
                ln2_stats(31)
                bcast(L + 2, L + 1)
                for c in range(C):
                    t.wait_ge(S["s_app"], a2_cnt(p, c))
                    if c == 0:
                        t.wait_ge(S["s_fin"], L + 2)
                    t.matmul(PH[0:1, 0:NT], ONEC[:, 0:1], xpc(p, c),
                             start=(c == 0), stop=(c == C - 1))
                    t.wait_ge(S["s_sqr"], jsq(p, 3, c) + 1)
                    t.matmul(PH[32:33, 0:NT], ONEC[:, 0:1], sqs(jsq(p, 3, c)),
                             start=(c == 0), stop=(c == C - 1)).then_inc(S["s_psq"], 1)
                bcast(L + 3, L + 2)
                if p > 0:
                    t.wait_ge(S["s_sm"], SM * p)
                    t.wait_ge(S["s_fin"], L + 1)
                last = None
                for c in range(C):
                    t.wait_ge(S["s_app"], a3_cnt(p, c))
                    last = t.matmul(PX[0:8, 0:NT], WRTB[:, 8 * c:8 * (c + 1)], uch(c),
                                    start=(c == 0), stop=(c == C - 1))
                last.then_inc(S["s_rlg"], 1)
                t.wait_ge(S["s_fin"], L + 3)
                if p > 0:
                    t.wait_ge(S["s_hwv"], p)
                last = None
                for c in range(C):
                    last = t.matmul(PH[:, 0:NT], ATB[:, 128 * c:128 * (c + 1)], uch(c),
                                    start=(c == 0), stop=(c == C - 1))
                last.then_inc(S["s_hmm"], 1)
                t.wait_ge(S["s_sm"], SM * p + 1)
                last = None
                for gg in range(4):
                    last = t.transpose(PX[:, 8 * gg:8 * (gg + 1)],
                                       lgsb[:, 128 * gg:128 * (gg + 1)], IDB[0:8, 0:8])
                last.then_inc(S["s_tp1"], 1)

                def f_base(o):
                    k = C * p + o
                    t.wait_ge(S[f"s_slw_{k % 2}"], 16 * (k // 2 + 1))
                    t.wait_ge(S["s_evac"], 64 * p + 29 + o)
                    for s in range(C):
                        t.matmul(PB[o % 4][:, 0:NT], slw_t(o % 2, s), uch(s),
                                 start=(s == 0), stop=False)

                def f_expert(o):
                    t.matmul(PB[o % 4][:, 0:NT], BTB[:, 128 * o:128 * (o + 1)], HW[:, :],
                             start=False, stop=True).then_inc(S["s_ymm"], 1)

                def ln4_stats(c):
                    t.wait_ge(S["s_evac"], 64 * p + 32 + c + 1)
                    if c == 0:
                        t.wait_ge(S["s_fin"], L + 3)
                        t.wait_ge(S["s_hwv"], p + 1)
                    t.matmul(PH[0:1, 0:NT], ONEC[:, 0:1], abch(c),
                             start=(c == 0), stop=(c == C - 1))
                    t.wait_ge(S["s_sqr"], jsq(p, 4, c) + 1)
                    t.matmul(PH[32:33, 0:NT], ONEC[:, 0:1], sqs(jsq(p, 4, c)),
                             start=(c == 0), stop=(c == C - 1)).then_inc(S["s_psq"], 1)
                f_base(0)
                f_base(1)
                t.wait_ge(S["s_sm"], SM * p + 11)
                last = None
                for gg in range(4):
                    last = t.transpose(PX[0:8, 128 * gg:128 * (gg + 1)],
                                       comb[:, 8 * gg:8 * (gg + 1)], IDB[:, :])
                last.then_inc(S["s_tp2"], 1)
                t.wait_ge(S["s_sm"], SM * p + 12)
                t.matmul(PX[:, 0:NT], E16B[:, :], cfm, start=True, stop=True
                         ).then_inc(S["s_cwb"], 1)
                t.wait_ge(S["s_hwv"], p + 1)
                f_expert(0)
                f_expert(1)
                ln4_stats(0)
                for o in range(2, C):
                    f_base(o)
                    f_expert(o)
                    ln4_stats(o - 1)
                    if p + 1 < NP:
                        ln1_stats(p + 1, o - 2)
                ln4_stats(31)
                if p + 1 < NP:
                    ln1_stats(p + 1, 30)
                    ln1_stats(p + 1, 31)
                bcast(L + 4, L + 3)
                if p + 1 < NP:
                    bcast(L + 5, L + 4)

        @block.scalar
        def _(s):
            for c in range(C):
                s.wait_ge(xin_sem(0, c // 8), 16)
                if c >= 2:
                    s.wait_ge(S["s_psq1"], c - 1)
                s.activation(sq1s(c), xpc(0, c), AF.Square).then_inc(S["s_sqr1"], 1)
            for p in range(NP):
                L = 4 * p
                if p == 0:
                    s.wait_ge(S["s_fin"], 1)
                    s.activation(var, var, AF.Ln)
                    s.activation(rstd, var, AF.Exp, scale=-0.5).then_inc(S["s_rstd"], 1)
                if p == 0:
                    s.wait_ge(S["s_fmm"], C)
                    s.wait_ge(S["s_ssm"], 64)
                for tt in range(C):
                    s.wait_ge(S["s_ymm"], 64 * p + tt + 1)
                    if p > 0:
                        s.wait_ge(S["s_app"], a4_cnt(p - 1, tt))
                    s.activation(abch(tt), PB[tt % 4][:, 0:NT], AF.Gelu,
                                 bias=B2C[:, tt:tt + 1]).then_inc(S["s_evac"], 1)
                    j = jsq(p, 2, tt)
                    if j - 2 >= 0:
                        s.wait_ge(S["s_psq"], j - 1)
                    s.activation(sqs(j), abch(tt), AF.Square).then_inc(S["s_sqr"], 1)
                s.wait_ge(S["s_fin"], L + 2)
                s.wait_ge(S["s_bc"], L + 1)
                s.activation(var, var, AF.Ln)
                s.activation(rstd, var, AF.Exp, scale=-0.5).then_inc(S["s_rstd"], 1)
                for c in range(C):
                    s.wait_ge(S["s_app"], a2_cnt(p, c))
                    j = jsq(p, 3, c)
                    s.wait_ge(S["s_psq"], j - 1)
                    s.activation(sqs(j), xpc(p, c), AF.Square).then_inc(S["s_sqr"], 1)
                s.wait_ge(S["s_fin"], L + 3)
                s.wait_ge(S["s_bc"], L + 2)
                s.activation(var, var, AF.Ln)
                s.activation(rstd, var, AF.Exp, scale=-0.5).then_inc(S["s_rstd"], 1)
                s.wait_ge(S["s_sm"], SM * p + 8)
                s.activation(elm, lm, AF.Exp)
                s.activation(e2v, M2, AF.Exp).then_inc(S["s_rexp"], 1)
                for o in range(C):
                    s.wait_ge(S["s_ymm"], 64 * p + 32 + o + 1)
                    s.activation(abch(o), PB[o % 4][:, 0:NT], AF.Identity,
                                 bias=BBC[:, o:o + 1]).then_inc(S["s_evac"], 1)
                    j = jsq(p, 4, o)
                    s.wait_ge(S["s_psq"], j - 1)
                    s.activation(sqs(j), abch(o), AF.Square).then_inc(S["s_sqr"], 1)
                    if p + 1 < NP:
                        c = o
                        s.wait_ge(xin_sem(p + 1, c // 8), 16 * ((p + 1) // 2 + 1))
                        s.wait_ge(S["s_psq1"], 32 * (p + 1) + c - 1)
                        s.activation(sq1s(c), xpc(p + 1, c), AF.Square).then_inc(S["s_sqr1"], 1)
                s.wait_ge(S["s_fin"], L + 4)
                s.wait_ge(S["s_bc"], L + 3)
                s.activation(var, var, AF.Ln)
                s.activation(rstd, var, AF.Exp, scale=-0.5).then_inc(S["s_rstd"], 1)
                if p + 1 < NP:
                    s.wait_ge(S["s_fin"], L + 5)
                    s.wait_ge(S["s_bc"], L + 4)
                    s.activation(var, var, AF.Ln)
                    s.activation(rstd, var, AF.Exp, scale=-0.5).then_inc(S["s_rstd"], 1)

        @block.vector
        def _(v):
            v.memset(ONEC[:, :], 1.0)
            v.memset(ONER[:, :], 1.0).then_inc(S["s_ones"], 1)
            for sc in range(C):
                v.wait_ge(S["s_fmm"], sc + 1)
                v.wait_ge(S[f"s_msk_{sc % 2}"], 16 * (sc // 2 + 1))
                if sc >= 2:
                    v.wait_ge(S[f"s_fst_{sc % 2}"], 16 * 4 * (sc // 2))
                v.tensor_tensor(FEV[:, NT * (sc % 2):NT * (sc % 2 + 1)],
                                PB[sc % 4][:, 0:NT], MK[sc % 2],
                                OP.mult).then_inc(S["s_fev"], 1)
            smc = [0]

            def sm_inc(ins):
                smc[0] += 1
                ins.then_inc(S["s_sm"], 1)

            def sm_step(ins):
                sm_inc(ins)
                v.wait_ge(S["s_sm"], smc[0])

            def finalize(bank, gl):
                if gl > 0:
                    v.wait_ge(S["s_bc"], gl)
                v.tensor_scalar(mean, bank[0:1, 0:NT], INV_D, None, OP.mult)
                v.tensor_scalar(var, bank[32:33, 0:NT], INV_D, EPS, OP.mult, OP.add)
                v.tensor_tensor(msq, mean, mean, OP.mult)
                v.tensor_tensor(var, var, msq, OP.subtract).then_inc(S["s_fin"], 1)

            def copies(mb, rb):
                v.tensor_copy(mb, PM[:, 0:NT])
                v.tensor_copy(rb, PR[:, 0:NT]).then_inc(S["s_mbc"], 1)

            def apply1_stream(p):
                for c in range(C):
                    v.wait_ge(xin_sem(p, c // 8), 16 * (p // 2 + 1))
                    v.tensor_tensor(T1[:, :], xpc(p, c), MB2, OP.subtract)
                    v.tensor_tensor(uch(c), T1[:, :], RB2, OP.mult).then_inc(S["s_app"], 1)

            v.wait_ge(S["s_psq1"], C)
            v.wait_ge(S["s_fst_0"], 16 * 4 * (C // 2))
            v.wait_ge(S["s_fst_1"], 16 * 4 * (C // 2))
            finalize(PX, 0)
            v.wait_ge(S["s_bc"], 1)
            copies(MB2, RB2)
            apply1_stream(0)
            for p in range(NP):
                L = 4 * p
                v.wait_ge(S["s_psq"], 96 * p + 32)
                finalize(PH, L + 1)
                v.wait_ge(S["s_bc"], L + 2)
                copies(MB[:, :], RB[:, :])
                for c in range(C):
                    v.tensor_tensor(T1[:, :], abch(c), MB[:, :], OP.subtract)
                    v.tensor_tensor(T1[:, :], T1[:, :], RB[:, :], OP.mult)
                    v.tensor_tensor(xpc(p, c), xpc(p, c), T1[:, :], OP.add).then_inc(S["s_app"], 1)
                v.wait_ge(S["s_psq"], 96 * p + 64)
                finalize(PH, L + 2)
                v.wait_ge(S["s_bc"], L + 3)
                copies(MB[:, :], RB[:, :])
                for c in range(C):
                    v.tensor_tensor(T1[:, :], xpc(p, c), MB[:, :], OP.subtract)
                    v.tensor_tensor(uch(c), T1[:, :], RB[:, :], OP.mult).then_inc(S["s_app"], 1)
                v.wait_ge(S["s_rlg"], p + 1)
                sm_inc(v.tensor_copy(lgsb, PX[0:8, 0:NT]))
                v.wait_ge(S["s_tp1"], p + 1)
                sm_step(v.tensor_copy(lgtm, PX[:, 0:32]))
                sm_step(v.reduce_max(M1, lgtm.rearrange("p (g e) -> p g e", e=8), axis=AX.X))
                last = None
                for gg in range(4):
                    last = v.tensor_scalar(lm[:, 8 * gg:8 * (gg + 1)],
                                           lgtm[:, 8 * gg:8 * (gg + 1)],
                                           M1[:, gg:gg + 1], None, OP.subtract)
                sm_step(last)
                sm_step(v.tensor_scalar(mskS, lm, 0.0, None, OP.is_equal))
                sm_step(v.scalar_tensor_tensor(mskS, mskS, -1e30, lm, OP.mult, OP.add))
                sm_step(v.reduce_max(M2, mskS.rearrange("p (g e) -> p g e", e=8), axis=AX.X))
                last = None
                for gg in range(4):
                    last = v.tensor_scalar(sel[:, 8 * gg:8 * (gg + 1)],
                                           lm[:, 8 * gg:8 * (gg + 1)],
                                           M2[:, gg:gg + 1], None, OP.is_ge)
                sm_inc(last)
                v.wait_ge(S["s_rexp"], p + 1)
                sm_step(v.tensor_scalar(e2v, e2v, 0.5, 0.5, OP.mult, OP.add))
                sm_step(v.reciprocal(rec, e2v))
                last = None
                for gg in range(4):
                    last = v.scalar_tensor_tensor(comb[:, 8 * gg:8 * (gg + 1)],
                                                  elm[:, 8 * gg:8 * (gg + 1)],
                                                  rec[:, gg:gg + 1],
                                                  sel[:, 8 * gg:8 * (gg + 1)],
                                                  OP.mult, OP.mult)
                sm_inc(last)
                v.wait_ge(S["s_tp2"], p + 1)
                sm_inc(v.tensor_copy(cfm, PX[0:8, 0:NT]))
                v.wait_ge(S["s_cwb"], p + 1)
                sm_inc(v.tensor_copy(CW, PX[:, 0:NT]))
                v.wait_ge(S["s_hmm"], p + 1)
                v.tensor_tensor(HW[:, :], PH[:, 0:NT], CW, OP.mult).then_inc(S["s_hwv"], 1)
                v.wait_ge(S["s_psq"], 96 * p + 96)
                finalize(PH, L + 3)
                v.wait_ge(S["s_bc"], L + 4)
                copies(MB[:, :], RB[:, :])
                if p + 1 < NP:
                    v.wait_ge(S["s_psq1"], 32 * (p + 1) + 32)
                    finalize(PX, L + 4)
                    v.wait_ge(S["s_bc"], L + 5)
                    copies(MB2, RB2)
                    apply1_stream(p + 1)
                for c in range(C):
                    v.tensor_tensor(T1[:, :], abch(c), MB[:, :], OP.subtract)
                    v.tensor_tensor(T1[:, :], T1[:, :], RB[:, :], OP.mult)
                    if 8 * p + c // 4 > 0:
                        v.wait_ge(S[f"s_out_{c % 4}"], 16 * (8 * p + c // 4))
                    v.tensor_tensor(OUTS[:, NT * (c % 4):NT * (c % 4 + 1)],
                                    xpc(p, c), T1[:, :], OP.add).then_inc(S["s_app"], 1)
            assert smc[0] == SM * NP, smc[0]
    return nc


def kernel(L0_attn_x, L0_attn_w1, L0_attn_w2, L0_attn_b2, L0_attn_sparse_mask,
           L0_ffn_W_base, L0_ffn_b_base, L0_ffn_W_router, L0_ffn_A, L0_ffn_B):
    x = np.asarray(L0_attn_x, dtype=np.float32)
    w1_ = np.asarray(L0_attn_w1, dtype=np.float32)
    w1 = np.ascontiguousarray(w1_.reshape(C, 128, C, 128).transpose(2, 1, 0, 3).reshape(C, 128, D))
    w2t = np.ascontiguousarray(np.asarray(L0_attn_w2, dtype=np.float32).T)
    mask = np.ascontiguousarray(np.asarray(L0_attn_sparse_mask, dtype=np.float32))
    wbt_ = np.asarray(L0_ffn_W_base, dtype=np.float32).T
    wbt = np.ascontiguousarray(wbt_.reshape(C, 128, C, 128).transpose(2, 1, 0, 3).reshape(C, 128, D))
    b2c = np.ascontiguousarray(np.asarray(L0_attn_b2, dtype=np.float32).reshape(C, 128).T)
    bbc = np.ascontiguousarray(np.asarray(L0_ffn_b_base, dtype=np.float32).reshape(C, 128).T)
    A = np.asarray(L0_ffn_A, dtype=np.float32)
    B = np.asarray(L0_ffn_B, dtype=np.float32)
    Wr = np.asarray(L0_ffn_W_router, dtype=np.float32)
    at_r = np.ascontiguousarray(
        A.reshape(128, D).T.reshape(C, 128, 128).transpose(1, 0, 2).reshape(128, D))
    bt_r = np.ascontiguousarray(B.transpose(0, 2, 1).reshape(128, D))
    wrt_r = np.ascontiguousarray(
        Wr.T.reshape(C, 128, 8).transpose(1, 0, 2).reshape(128, 256))
    e16 = np.zeros((8, 128), dtype=np.float32)
    for e in range(8):
        e16[e, e * 16:(e + 1) * 16] = 1.0
    ident = np.eye(128, dtype=np.float32)

    nc = build_nc()
    in_maps = []
    for r in range(NCORE):
        xs = np.ascontiguousarray(x[0, r * TPC:(r + 1) * TPC, :].T)
        w2t_r = np.ascontiguousarray(
            w2t[:, r * 512:(r + 1) * 512].reshape(C, 128, 512).transpose(1, 0, 2).reshape(128, C * 512))
        in_maps.append({
            "x_fm": xs, "w1_r": w1, "w2t_r": w2t_r,
            "mask_slc": np.ascontiguousarray(mask[:, r * 512:(r + 1) * 512]),
            "wbt_r": wbt, "at_r": at_r, "bt_r": bt_r, "wrt_r": wrt_r,
            "b2c": b2c, "bbc": bbc, "e16": e16, "ident": ident,
        })
    res = run_bass_kernel_spmd(nc, in_maps, core_ids=list(range(NCORE)))
    out = np.empty((1, NCORE * TPC, D), dtype=np.float32)
    for r in range(NCORE):
        out[0, r * TPC:(r + 1) * TPC, :] = res.results[r]["out_fm"].T
    return out
